# version 06
# baseline (speedup 1.0000x reference)
import numpy as np
import concourse.bass as bass
import concourse.mybir as mybir
from concourse.bass_utils import run_bass_kernel_spmd

F32 = mybir.dt.float32
BF16 = mybir.dt.bfloat16
AF = mybir.ActivationFunctionType
ALU = mybir.AluOpType
AX = mybir.AxisListType.X

NCORE = 8
L = 2048
D = 1024
NT = L // 128
NDC = D // 128
NE = 16
NU = 18
TG = 256
NTG = L // TG
NH = 16
DH = 64
NEG = -1e30
EPS = 1e-5
STOP_AFTER = None
DEBUG_SERIAL = False
DEBUG_DUMP = False


class Op:
    __slots__ = ("eng", "fn", "waits", "incs", "incs_after", "dma")

    def __init__(self, eng, fn, waits, incs, incs_after, dma=False):
        self.eng = eng
        self.fn = fn
        self.waits = waits
        self.incs = incs
        self.incs_after = incs_after
        self.dma = dma


class B:
    def __init__(self):
        self.ops = []
        self.cnt = {}
        self.sems = {}
        self.semnames = []

    def sem(self, name):
        if name not in self.cnt:
            self.cnt[name] = 0
            self.semnames.append(name)
        return name

    def emit(self, eng, fn, waits=(), incs=(), dma=False):
        for s, n in incs:
            self.sem(s)
        ws = [(s, v) for (s, v) in waits if v > 0]
        for s, v in ws:
            self.sem(s)
        op = Op(eng, fn, ws, list(incs), [], dma)
        self.ops.append(op)
        for s, n in incs:
            self.cnt[s] += n
            op.incs_after.append((s, self.cnt[s]))

    def now(self, s):
        return self.cnt.get(s, 0)


def build_program():
    nc = bass.Bass("TRN2", target_bir_lowering=False, detect_race_conditions=False)

    def din(name, shape, dtype=F32):
        return nc.dram_tensor(name, shape, dtype, kind="ExternalInput").ap()

    x_in = din("x", [L, D])
    wqT = din("wqT", [D, D]); wkT = din("wkT", [D, D])
    wvT = din("wvT", [D, D]); woT = din("woT", [D, D])
    bo_in = din("bo", [1, D])
    gq_in = din("gq_b", [128, DH]); bq_in = din("bq_b", [128, DH])
    gk_in = din("gk_b", [128, DH]); bk_in = din("bk_b", [128, DH])
    gwT = din("gwT", [D, NE])
    gb_in = din("gb_b", [128, NE])
    we1 = din("we1", [NE, D, 512]); we3 = din("we3", [NE, D, 512])
    we2 = din("we2", [NE, 512, D])
    ws1 = din("ws1", [D, D]); ws3 = din("ws3", [D, D]); ws2 = din("ws2", [D, D])
    becol1 = din("becol1", [128, NE, 4]); becol3 = din("becol3", [128, NE, 4])
    bscol1 = din("bscol1", [128, 8]); bscol3 = din("bscol3", [128, 8])
    be2_in = din("be2", [NE, D])
    bs2_in = din("bs2", [1, D])
    oneh_in = din("oneh", [NE, L])
    mask_tm_in = din("mask_tm", [128, 128])
    mask_ut_in = din("mask_ut", [128, 128])
    ident_in = din("ident", [128, 128])
    out = nc.dram_tensor("out", [L, D], F32, kind="ExternalOutput").ap()

    qs_d = nc.dram_tensor("qs_d", [D, L], BF16, kind="Internal").ap()
    ks_d = nc.dram_tensor("ks_d", [D, L], BF16, kind="Internal").ap()
    vs_d = nc.dram_tensor("vs_d", [L, D], BF16, kind="Internal").ap()

    import contextlib
    _stk = contextlib.ExitStack()
    with _stk:
        x_sb = _stk.enter_context(nc.sbuf_tensor("x_sb", [128, NT * D], F32))
        stage32 = _stk.enter_context(nc.sbuf_tensor("stage32", [128, 2048], F32))
        scr32 = _stk.enter_context(nc.sbuf_tensor("scr32", [128, 1024], F32))
        stat32 = _stk.enter_context(nc.sbuf_tensor("stat32", [128, 192], F32))
        gqb = _stk.enter_context(nc.sbuf_tensor("gqb", [128, DH], F32))
        bqb = _stk.enter_context(nc.sbuf_tensor("bqb", [128, DH], F32))
        gkb = _stk.enter_context(nc.sbuf_tensor("gkb", [128, DH], F32))
        bkb = _stk.enter_context(nc.sbuf_tensor("bkb", [128, DH], F32))
        gbb = _stk.enter_context(nc.sbuf_tensor("gbb", [128, NE], F32))
        bec1 = _stk.enter_context(nc.sbuf_tensor("bec1", [128, NE, 4], F32))
        bec3 = _stk.enter_context(nc.sbuf_tensor("bec3", [128, NE, 4], F32))
        bsc1 = _stk.enter_context(nc.sbuf_tensor("bsc1", [128, 8], F32))
        bsc3 = _stk.enter_context(nc.sbuf_tensor("bsc3", [128, 8], F32))
        mtm = _stk.enter_context(nc.sbuf_tensor("mtm", [128, 128], F32))
        mut = _stk.enter_context(nc.sbuf_tensor("mut", [128, 128], F32))
        idn = _stk.enter_context(nc.sbuf_tensor("idn", [128, 128], F32))
        gsc = _stk.enter_context(nc.sbuf_tensor("gsc", [128, 1792], F32))
        xn_fm = _stk.enter_context(nc.sbuf_tensor("xn_fm", [128, NDC, L], BF16))
        tmr = _stk.enter_context(nc.sbuf_tensor("tmr", [128, 2, D], BF16))
        u16 = _stk.enter_context(nc.sbuf_tensor("u16", [128, 37120], BF16))
        be2b = _stk.enter_context(nc.sbuf_tensor("be2b", [NE, D], BF16))
        bs2row = _stk.enter_context(nc.sbuf_tensor("bs2row", [1, D], BF16))
        borow = _stk.enter_context(nc.sbuf_tensor("borow", [1, D], BF16))
        onesbf = _stk.enter_context(nc.sbuf_tensor("onesbf", [1, 128], BF16))
        gwTs = _stk.enter_context(nc.sbuf_tensor("gwTs", [128, NDC, NE], BF16))
        silu_r = _stk.enter_context(nc.sbuf_tensor("silu_r", [128, 2, TG], BF16))
        h3s_r = _stk.enter_context(nc.sbuf_tensor("h3s_r", [128, 2, TG], BF16))
        p0 = _stk.enter_context(nc.psum_tensor("p0", [128, 512], F32))
        p1 = _stk.enter_context(nc.psum_tensor("p1", [128, 512], F32))
        p2 = _stk.enter_context(nc.psum_tensor("p2", [128, 512], F32))
        p3 = _stk.enter_context(nc.psum_tensor("p3", [128, 512], F32))
        p4 = _stk.enter_context(nc.psum_tensor("p4", [128, 512], F32))
        p5 = _stk.enter_context(nc.psum_tensor("p5", [128, 512], F32))
        p6 = _stk.enter_context(nc.psum_tensor("p6", [128, 512], F32))
        p7 = _stk.enter_context(nc.psum_tensor("p7", [128, 512], F32))
        blk = _stk.enter_context(nc.Block())
        rb32 = scr32[:, 0:512]
        vstg = stage32[:, 0:512].bitcast(BF16)
        fmstg = [stage32[:, 512:1024].bitcast(BF16),
                 stage32[:, 1280:1792].bitcast(BF16)]
        rrow16 = stage32[0:1, 1024:1280].bitcast(BF16)
        rrow32 = stage32[0:1, 1280:1792]
        pb = [p0, p1, p2, p3, p4, p5, p6, p7]
        b = B()

        def W1(s):
            return u16[:, s * 12288: s * 12288 + 4096].rearrange("p (c i) -> p c i", c=8)

        def W3(s):
            return u16[:, s * 12288 + 4096: s * 12288 + 8192].rearrange("p (c i) -> p c i", c=8)

        def W2(s):
            return u16[:, s * 12288 + 8192: s * 12288 + 12288].rearrange("p (c i) -> p c i", c=4)

        def HS(s):
            return u16[:, 24576 + s * 1024: 24576 + (s + 1) * 1024]

        def CBC(s):
            return u16[:, 26624 + s * 2048: 26624 + (s + 1) * 2048]

        C_km = u16[0:NE, 30720:32768]
        onehot = u16[0:NE, 32768:34816]

        def QPair(s):
            return u16[:, s * 2048: (s + 1) * 2048]

        def KPair(s):
            return u16[:, 4096 + s * 2048: 4096 + (s + 1) * 2048]

        def VPair(s):
            return u16[:, 8192 + s * 2080: 8192 + (s + 1) * 2080].rearrange(
                "p (c w) -> p c w", c=16)

        def PP(s):
            return u16[:, 12352 + s * 512: 12352 + (s + 1) * 512]

        def WT(s):
            return u16[:, 14400 + s * 8192: 14400 + (s + 1) * 8192].rearrange(
                "p (c o) -> p c o", c=8)

        nmrow = u16[0:1, 34816:36864]
        nmT_sb = u16[0:16, 36864:36992]

        s1c = stat32[:, 0:16]; s2c = stat32[:, 16:32]
        mc = stat32[:, 32:48]; rc = stat32[:, 48:64]
        qs1 = stat32[:, 64:80]; qs2 = stat32[:, 80:96]
        qm = stat32[:, 96:112]; qr = stat32[:, 112:128]
        nmc = stat32[:, 128:144]
        tmp4 = stat32[:, 144:148]
        lnt = stat32[:, 148:149]
        qmsq = stat32[:, 150:166]

        g_exp = gsc[:, 0:256]; g_orig = gsc[:, 256:512]
        g_sc2 = gsc[:, 512:768]; g_v2 = gsc[:, 768:1024]
        g_kn = gsc[:, 1024:1280]; g_Cf = gsc[:, 1280:1536]
        g_m1g = gsc[:, 1536:1600]; g_t = gsc[:, 1600:1664]
        g_gs = gsc[:, 1664:1728]; g_sm = gsc[:, 1728:1792]

        LD = "ld"; LDG = "ldg"
        WL = ["wla", "wlb"]
        TS = ["ts0", "ts1", "ts2", "ts3"]
        SP = ["spa", "spb"]
        VSP = "vsp"
        HP = ["hpa", "hpb"]
        HD = "hd"; AD = "ad"; HSF = "hsf"; YD = "yd"; YF = "yf"
        CBD = "cbd"; CBE = "cbe"; GMM = "gmm"; GPP = "gpp"
        SETE = "sete"; LNP = "lnp"
        QKD = "qkd"; QKL = "qkl"
        STM = "stm"; SMK = "smk"; NMT = "nmt"; SKM = "skm"; KMM = "kmm"
        EXD = "exd"; AVD = "avd"; RCD = "rcd"; XAT = "xat"
        ST = "st"
        for nm in (LD, LDG, WL[0], WL[1], TS[0], TS[1], TS[2], TS[3],
                   SP[0], SP[1], VSP,
                   HP[0], HP[1], HD, AD, HSF, YD, YF, CBD, CBE, GMM, GPP,
                   SETE, LNP, QKD, QKL, STM, SMK, NMT, SKM,
                   KMM, EXD, AVD, RCD, XAT, ST):
            b.sem(nm)

        def ld_sync(dst_ap, src_ap, waits=()):
            b.emit("sync", lambda e, d=dst_ap, s=src_ap: e.dma_start(d, s),
                   waits=waits, incs=[(LD, 16)])

        def ld_gp(dst_ap, src_ap, waits=()):
            b.emit("gpsimd", lambda e, d=dst_ap, s=src_ap: e.dma_start(d, s),
                   waits=waits, incs=[(LDG, 16)])

        ld_sync(x_sb[:].rearrange("p (t d) -> p t d", t=NT),
                x_in.rearrange("(t p) d -> p t d", p=128))
        ld_sync(bec1[:], becol1); ld_sync(bec3[:], becol3)
        ld_sync(bsc1[:], bscol1); ld_sync(bsc3[:], bscol3)
        ld_sync(mtm[:], mask_tm_in); ld_sync(mut[:], mask_ut_in)
        ld_sync(idn[:], ident_in)
        ld_sync(gqb[:], gq_in); ld_sync(bqb[:], bq_in)
        ld_sync(gkb[:], gk_in); ld_sync(bkb[:], bk_in)
        ld_sync(gbb[:], gb_in)
        ld_gp(gwTs[:], gwT.rearrange("(c p) e -> p c e", p=128))
        ld_gp(be2b[:], be2_in)
        ld_gp(bs2row[:], bs2_in)
        ld_gp(borow[:], bo_in)
        ld_gp(onehot, oneh_in)
        n_ld = b.now(LD)
        n_ldg = b.now(LDG)

        b.emit("vector", lambda e: e.memset(onesbf[0:1, :], 1.0), incs=[(SETE, 1)])
        b.emit("vector", lambda e: e.tensor_scalar(
            be2b[:], be2b[:], 0.5, None, op0=ALU.mult),
            waits=[(LDG, n_ldg)], incs=[(SETE, 1)])
        b.emit("vector", lambda e: e.tensor_scalar(
            bs2row[:], bs2row[:], 0.5, None, op0=ALU.mult), incs=[(SETE, 1)])
        b.emit("vector", lambda e: e.tensor_scalar(
            bsc3[:], bsc3[:], 0.5, None, op0=ALU.mult),
            waits=[(LD, n_ld)], incs=[(SETE, 1)])

        def ln_phase():
            nyf = b.now(YF)
            nxat = b.now(XAT)
            for ti in range(NT):
                xt = x_sb[:, ti * D:(ti + 1) * D]
                sg = stage32[:, (ti % 2) * 1024:(ti % 2) * 1024 + 1024]
                tmt = tmr[:, ti % 2, :]
                b.emit("vector", lambda e, xt=xt, ti=ti: e.reduce_sum(
                    s1c[:, ti:ti + 1], xt, axis=AX),
                    waits=[(LD, n_ld), (YF, nyf), (XAT, nxat)], incs=[(LNP, 1)])
                b.emit("scalar", lambda e, xt=xt, sg=sg: e.activation(
                    sg, xt, AF.Square),
                    waits=[(LNP, b.now(LNP))],
                    incs=[(LNP, 1)])
                b.emit("vector", lambda e, sg=sg, ti=ti: e.reduce_sum(
                    s2c[:, ti:ti + 1], sg, axis=AX),
                    waits=[(LNP, b.now(LNP))], incs=[(LNP, 1)])
                b.emit("vector", lambda e, ti=ti: e.tensor_scalar(
                    mc[:, ti:ti + 1], s1c[:, ti:ti + 1], 1.0 / D, None, op0=ALU.mult),
                    incs=[(LNP, 1)])
                b.emit("vector", lambda e, ti=ti: e.tensor_scalar(
                    rc[:, ti:ti + 1], s2c[:, ti:ti + 1], 1.0 / D, None, op0=ALU.mult),
                    incs=[(LNP, 1)])
                b.emit("vector", lambda e, ti=ti: e.tensor_tensor(
                    lnt[:, 0:1], mc[:, ti:ti + 1], mc[:, ti:ti + 1], op=ALU.mult),
                    incs=[(LNP, 1)])
                b.emit("vector", lambda e, ti=ti: e.tensor_sub(
                    rc[:, ti:ti + 1], rc[:, ti:ti + 1], lnt[:, 0:1]),
                    incs=[(LNP, 1)])
                b.emit("vector", lambda e, ti=ti: e.tensor_scalar(
                    rc[:, ti:ti + 1], rc[:, ti:ti + 1], EPS, None, op0=ALU.add),
                    incs=[(LNP, 1)])
                b.emit("scalar", lambda e, ti=ti: e.activation(
                    rc[:, ti:ti + 1], rc[:, ti:ti + 1], AF.Sqrt),
                    waits=[(LNP, b.now(LNP))], incs=[(LNP, 1)])
                b.emit("vector", lambda e, ti=ti: e.reciprocal(
                    rc[:, ti:ti + 1], rc[:, ti:ti + 1]),
                    waits=[(LNP, b.now(LNP))], incs=[(LNP, 1)])
                gprev = TS[((ti - 2) % NT) // 4]
                wtm = [(gprev, b.now(gprev))]
                b.emit("vector", lambda e, xt=xt, tmt=tmt, ti=ti: e.tensor_scalar(
                    tmt, xt, mc[:, ti:ti + 1], rc[:, ti:ti + 1],
                    op0=ALU.subtract, op1=ALU.mult),
                    waits=wtm, incs=[(LNP, 1)])
                nlnp = b.now(LNP)
                g = TS[ti // 4]
                for dc in range(NDC):
                    b.emit("sync", lambda e, tmt=tmt, dc=dc, ti=ti: e.dma_start_transpose(
                        xn_fm[:, dc, ti * 128:(ti + 1) * 128],
                        tmt[:, dc * 128:(dc + 1) * 128]),
                        waits=[(LNP, nlnp)] if dc == 0 else [], incs=[(g, 16)])

        def gate_phase():
            nyf = b.now(YF)
            nxat = b.now(XAT)
            nsete = b.now(SETE)
            for ti in range(NT):
                for dc in range(NDC):
                    w = []
                    if dc == 0:
                        w = [(TS[ti // 4], b.now(TS[ti // 4]))]
                        if ti == 0:
                            w += [(LDG, n_ldg), (YF, nyf), (XAT, nxat),
                                  (SETE, nsete)]
                    b.emit("tensor", lambda e, ti=ti, dc=dc: e.matmul(
                        p0[:, ti * 16:(ti + 1) * 16],
                        xn_fm[:, dc, ti * 128:(ti + 1) * 128],
                        gwTs[:, dc, :],
                        start=(dc == 0), stop=(dc == NDC - 1)),
                        waits=w,
                        incs=[(GMM, 1)] if (ti == NT - 1 and dc == NDC - 1) else [])
            b.emit("vector", lambda e: e.reduce_max(
                g_sm[:, 0:16], p0[:, 0:256].rearrange("p (t e) -> p t e", t=16),
                axis=AX),
                waits=[(GMM, b.now(GMM))], incs=[(GPP, 1)])
            if DEBUG_DUMP:
                b.emit("vector", lambda e: e.tensor_copy(
                    scr32[:, 0:256], p0[:, 0:256]), incs=[(GPP, 1)])
                b.emit("vector", lambda e: e.tensor_copy(
                    scr32[:, 256:272], g_sm[:, 0:16]), incs=[(GPP, 1)])
            b.emit("vector", lambda e: e.tensor_scalar(
                g_sm[:, 16:32], g_sm[:, 0:16], -1.0, None, op0=ALU.mult),
                incs=[(GPP, 1)])
            ngpp = b.now(GPP)
            for ti in range(NT):
                b.emit("scalar", lambda e, ti=ti: e.activation(
                    g_exp[:, ti * 16:(ti + 1) * 16], p0[:, ti * 16:(ti + 1) * 16],
                    AF.Exp, bias=g_sm[:, 16 + ti:17 + ti], scale=1.0),
                    waits=[(GPP, ngpp)] if ti == 0 else [],
                    incs=[(GPP, 1)] if ti == NT - 1 else [])
            b.emit("vector", lambda e: e.reduce_sum(
                g_sm[:, 32:48], g_exp[:].rearrange("p (t e) -> p t e", t=16), axis=AX),
                waits=[(GPP, b.now(GPP))], incs=[(GPP, 1)])
            b.emit("vector", lambda e: e.reciprocal(
                g_sm[:, 48:64], g_sm[:, 32:48]), incs=[(GPP, 1)])
            for ti in range(NT):
                b.emit("vector", lambda e, ti=ti: e.tensor_scalar(
                    g_orig[:, ti * 16:(ti + 1) * 16], g_exp[:, ti * 16:(ti + 1) * 16],
                    g_sm[:, 48 + ti:49 + ti], None, op0=ALU.mult),
                    incs=[(GPP, 1)])

            def v4(a):
                return a.rearrange("p (t g f) -> p t g f", t=16, g=4)

            def v3(a):
                return a.rearrange("p (t g) -> p t g", t=16)

            def e3(a):
                return a.rearrange("p (t e) -> p t e", t=16)

            def bc3(a, w):
                return a.rearrange("p (t o) -> p t o", o=1).broadcast_to([128, 16, w])

            def bc4(a):
                return a.rearrange("p (t g o) -> p t g o", t=16, g=4).broadcast_to(
                    [128, 16, 4, 4])

            b.emit("vector", lambda e: e.tensor_tensor(
                e3(g_sc2[:]), e3(g_orig[:]),
                gbb[:].rearrange("p (o e) -> p o e", o=1).broadcast_to([128, 16, 16]),
                op=ALU.add), incs=[(GPP, 1)])
            b.emit("vector", lambda e: e.reduce_max(
                v3(g_m1g[:]), v4(g_sc2[:]), axis=AX), incs=[(GPP, 1)])
            b.emit("vector", lambda e: e.tensor_tensor(
                v4(g_kn[:]), v4(g_sc2[:]), bc4(g_m1g[:]), op=ALU.is_ge),
                incs=[(GPP, 1)])
            b.emit("vector", lambda e: e.tensor_scalar(
                g_kn[:], g_kn[:], 1e30, None, op0=ALU.mult), incs=[(GPP, 1)])
            b.emit("vector", lambda e: e.tensor_sub(
                g_kn[:], g_sc2[:], g_kn[:]), incs=[(GPP, 1)])
            b.emit("vector", lambda e: e.reduce_max(
                v3(g_gs[:]), v4(g_kn[:]), axis=AX), incs=[(GPP, 1)])
            b.emit("vector", lambda e: e.tensor_add(
                g_gs[:], g_gs[:], g_m1g[:]), incs=[(GPP, 1)])
            b.emit("vector", lambda e: e.reduce_max(
                g_sm[:, 0:16], v3(g_gs[:]), axis=AX), incs=[(GPP, 1)])
            b.emit("vector", lambda e: e.tensor_tensor(
                v3(g_t[:]), v3(g_gs[:]), bc3(g_sm[:, 0:16], 4), op=ALU.is_ge),
                incs=[(GPP, 1)])
            b.emit("vector", lambda e: e.tensor_scalar(
                g_t[:], g_t[:], 1e30, None, op0=ALU.mult), incs=[(GPP, 1)])
            b.emit("vector", lambda e: e.tensor_sub(
                g_t[:], g_gs[:], g_t[:]), incs=[(GPP, 1)])
            b.emit("vector", lambda e: e.reduce_max(
                g_sm[:, 32:48], v3(g_t[:]), axis=AX), incs=[(GPP, 1)])
            b.emit("vector", lambda e: e.tensor_tensor(
                v3(g_t[:]), v3(g_gs[:]), bc3(g_sm[:, 32:48], 4), op=ALU.is_ge),
                incs=[(GPP, 1)])
            b.emit("vector", lambda e: e.tensor_scalar(
                g_t[:], g_t[:], 1e30, None, op0=ALU.mult), incs=[(GPP, 1)])
            b.emit("vector", lambda e: e.tensor_scalar(
                g_t[:], g_t[:], -1e30, None, op0=ALU.add), incs=[(GPP, 1)])
            b.emit("vector", lambda e: e.tensor_tensor(
                v4(g_v2[:]), v4(g_sc2[:]), bc4(g_t[:]), op=ALU.add), incs=[(GPP, 1)])
            b.emit("vector", lambda e: e.reduce_max(
                g_sm[:, 0:16], e3(g_v2[:]), axis=AX), incs=[(GPP, 1)])
            b.emit("vector", lambda e: e.tensor_tensor(
                e3(g_kn[:]), e3(g_v2[:]), bc3(g_sm[:, 0:16], 16), op=ALU.is_ge),
                incs=[(GPP, 1)])
            b.emit("vector", lambda e: e.tensor_scalar(
                g_kn[:], g_kn[:], 1e30, None, op0=ALU.mult), incs=[(GPP, 1)])
            b.emit("vector", lambda e: e.tensor_sub(
                g_kn[:], g_v2[:], g_kn[:]), incs=[(GPP, 1)])
            b.emit("vector", lambda e: e.reduce_max(
                g_sm[:, 32:48], e3(g_kn[:]), axis=AX), incs=[(GPP, 1)])
            b.emit("vector", lambda e: e.tensor_tensor(
                e3(g_kn[:]), e3(g_v2[:]), bc3(g_sm[:, 32:48], 16), op=ALU.is_ge),
                incs=[(GPP, 1)])
            b.emit("vector", lambda e: e.tensor_mul(
                g_Cf[:], g_kn[:], g_orig[:]), incs=[(GPP, 1)])
            ngv = b.now(GPP)
            cbe_hist = {}
            for w in range(4):
                bank = pb[4 + (w % 2)]
                for j in range(4):
                    ti = w * 4 + j
                    ww = []
                    if ti == 0:
                        ww.append((GPP, ngv))
                    if w >= 2 and j == 0:
                        ww.append((CBE, cbe_hist[w - 2]))
                    b.emit("tensor", lambda e, bank=bank, j=j, ti=ti: e.transpose(
                        bank[0:16, j * 128:(j + 1) * 128],
                        g_Cf[:, ti * 16:(ti + 1) * 16], idn[:]),
                        waits=ww, incs=[(CBD, 1)] if j == 3 else [])
                b.emit("vector", lambda e, bank=bank, w=w: e.tensor_scalar(
                    C_km[0:16, w * 512:(w + 1) * 512], bank[0:16, 0:512],
                    0.5, None, op0=ALU.mult),
                    waits=[(CBD, b.now(CBD))], incs=[(CBE, 1)])
                cbe_hist[w] = b.now(CBE)

        def ffn_phase(nunits=NU, bias=True):
            gate_phase()
            n_ckm = b.now(CBE)
            yf0 = b.now(YF)
            ad0 = b.now(AD)
            nxat = b.now(XAT)
            for u in range(nunits):
                slot = u % 2
                if u < 2:
                    wts = [(XAT, nxat)] if nxat > 0 else []
                else:
                    wts = [(YF, yf0 + 32 * (u - 1))]
                if u < NE:
                    src1 = we1[u].rearrange("(c p) i -> p c i", p=128)
                    src3 = we3[u].rearrange("(c p) i -> p c i", p=128)
                    src2 = we2[u].rearrange("(c p) d -> p c d", p=128)
                else:
                    hh_ = u - NE
                    src1 = ws1[:, hh_ * 512:(hh_ + 1) * 512].rearrange("(c p) i -> p c i", p=128)
                    src3 = ws3[:, hh_ * 512:(hh_ + 1) * 512].rearrange("(c p) i -> p c i", p=128)
                    src2 = ws2[hh_ * 512:(hh_ + 1) * 512, :].rearrange("(c p) d -> p c d", p=128)
                b.emit("gpsimd", lambda e, s=src1, slot=slot: e.dma_start(W1(slot), s),
                       waits=wts, incs=[(WL[slot], 16)])
                b.emit("gpsimd", lambda e, s=src3, slot=slot: e.dma_start(W3(slot), s),
                       incs=[(WL[slot], 16)])
                b.emit("gpsimd", lambda e, s=src2, slot=slot: e.dma_start(W2(slot), s),
                       incs=[(WL[slot], 16)])
                n_wld = b.now(WL[slot])

                if u < NE:
                    for j in range(4):
                        b.emit("tensor", lambda e, u=u, j=j: e.matmul(
                            pb[j][:, 0:512],
                            onehot[0:16, u * 128:(u + 1) * 128],
                            C_km[0:16, j * 512:(j + 1) * 512],
                            start=True, stop=True),
                            waits=([(CBE, n_ckm), (YF, b.now(YF))] if j == 0 else []),
                            incs=[(CBD, 1)])
                    ncb = b.now(CBD)
                    for j in range(4):
                        b.emit("vector", lambda e, slot=slot, j=j: e.tensor_copy(
                            CBC(slot)[:, j * 512:(j + 1) * 512], pb[j][:, 0:512]),
                            waits=[(CBD, ncb)] if j == 0 else [], incs=[(CBE, 1)])
                n_cbe = b.now(CBE)

                for tg in range(NTG):
                    tgc = tg * TG
                    for ic in range(4):
                        h1b = pb[4 + (ic % 2)]
                        h3b = pb[6 + (ic % 2)]
                        icg = (u * NTG + tg) * 4 + ic
                        wh = []
                        if ic == 0 and tg == 0:
                            wh.append((WL[slot], n_wld))
                        if icg >= 2:
                            wh.append((AD, ad0 + 2 * (icg - 1)))
                        for dc in range(NDC):
                            b.emit("tensor", lambda e, h1b=h1b, slot=slot, dc=dc, ic=ic, tgc=tgc: e.matmul(
                                h1b[:, 0:TG],
                                W1(slot)[:, dc, ic * 128:(ic + 1) * 128],
                                xn_fm[:, dc, tgc:tgc + TG],
                                start=(dc == 0), stop=(dc == NDC - 1)),
                                waits=wh if dc == 0 else [],
                                incs=[(HD, 1)] if dc == NDC - 1 else [])
                        for dc in range(NDC):
                            b.emit("tensor", lambda e, h3b=h3b, slot=slot, dc=dc, ic=ic, tgc=tgc: e.matmul(
                                h3b[:, 0:TG],
                                W3(slot)[:, dc, ic * 128:(ic + 1) * 128],
                                xn_fm[:, dc, tgc:tgc + TG],
                                start=(dc == 0), stop=(dc == NDC - 1)),
                                incs=[(HD, 1)] if dc == NDC - 1 else [])
                        nhd = b.now(HD)
                        if u < NE:
                            bia1 = bec1[:, u, ic:ic + 1]
                            bia3 = bec3[:, u, ic:ic + 1]
                            sc3 = 1.0
                        else:
                            jj = (u - NE) * 4 + ic
                            bia1 = bsc1[:, jj:jj + 1]
                            bia3 = bsc3[:, jj:jj + 1]
                            sc3 = 0.5
                        sil = silu_r[:, icg % 2, :]
                        h3s = h3s_r[:, icg % 2, :]
                        b.emit("scalar", lambda e, h1b=h1b, sil=sil, bia1=bia1: e.activation(
                            sil, h1b[:, 0:TG], AF.Silu, bias=bia1, scale=1.0),
                            waits=[(HD, nhd - 1), (HSF, b.now(HSF) - 1)],
                            incs=[(AD, 1)])
                        b.emit("scalar", lambda e, h3b=h3b, h3s=h3s, bia3=bia3, sc3=sc3: e.activation(
                            h3s, h3b[:, 0:TG], AF.Identity, bias=bia3, scale=sc3),
                            waits=[(HD, nhd)], incs=[(AD, 1)])
                        nad = b.now(AD)
                        if u < NE:
                            b.emit("vector", lambda e, h3s=h3s, slot=slot, tgc=tgc, ic=ic: e.tensor_mul(
                                HS(slot)[:, ic * TG:(ic + 1) * TG], h3s,
                                CBC(slot)[:, tgc:tgc + TG]),
                                waits=[(AD, nad), (CBE, n_cbe)])
                            b.emit("vector", lambda e, sil=sil, slot=slot, ic=ic: e.tensor_mul(
                                HS(slot)[:, ic * TG:(ic + 1) * TG],
                                HS(slot)[:, ic * TG:(ic + 1) * TG], sil),
                                incs=[(HSF, 1)])
                        else:
                            b.emit("vector", lambda e, h3s=h3s, sil=sil, slot=slot, ic=ic: e.tensor_mul(
                                HS(slot)[:, ic * TG:(ic + 1) * TG], h3s, sil),
                                waits=[(AD, nad)], incs=[(HSF, 1)])
                    nhsf = b.now(HSF)
                    fold_bias = False
                    for ic in range(4):
                        for tc in range(2):
                            for dh in range(2):
                                yb = pb[tc * 2 + dh]
                                w = []
                                if tc == 0 and dh == 0:
                                    w = [(HSF, nhsf - 3 + ic)]
                                    if ic == 0:
                                        w.append((YF, b.now(YF)))
                                        if u < NE:
                                            w.append((CBE, n_cbe))
                                b.emit("tensor", lambda e, yb=yb, slot=slot, ic=ic, tc=tc, dh=dh: e.matmul(
                                    yb[:, 0:512],
                                    HS(slot)[:, ic * TG + tc * 128: ic * TG + (tc + 1) * 128],
                                    W2(slot)[:, ic, dh * 512:(dh + 1) * 512],
                                    start=(ic == 0),
                                    stop=(ic == 3 and not fold_bias)),
                                    waits=w,
                                    incs=([(YD, 1)] if (ic == 3 and not fold_bias)
                                          else []))
                    if fold_bias:
                        for tc in range(2):
                            for dh in range(2):
                                yb = pb[tc * 2 + dh]
                                t0 = tg * TG + tc * 128
                                b.emit("tensor", lambda e, yb=yb, t0=t0, dh=dh: e.matmul(
                                    yb[:, 0:512], C_km[0:16, t0:t0 + 128],
                                    be2b[0:16, dh * 512:(dh + 1) * 512],
                                    start=False, stop=False))
                                b.emit("tensor", lambda e, yb=yb, dh=dh: e.matmul(
                                    yb[:, 0:512], onesbf[0:1, 0:128],
                                    bs2row[0:1, dh * 512:(dh + 1) * 512],
                                    start=False, stop=True), incs=[(YD, 1)])
                    nyd = b.now(YD)
                    for tc in range(2):
                        for dh in range(2):
                            yb = pb[tc * 2 + dh]
                            xt = x_sb[:, (tg * 2 + tc) * D + dh * 512:
                                      (tg * 2 + tc) * D + (dh + 1) * 512]
                            b.emit("vector", lambda e, yb=yb, xt=xt: e.tensor_tensor(
                                xt, yb[:, 0:512], xt, op=ALU.add),
                                waits=[(YD, nyd - 3 + tc * 2 + dh)], incs=[(YF, 1)])
            if not bias:
                return
            for tg in range(NTG):
                for tc in range(2):
                    for dh in range(2):
                        yb = pb[tc * 2 + dh]
                        t0 = tg * TG + tc * 128
                        w = [(YF, b.now(YF))] if (tc == 0 and dh == 0) else []
                        b.emit("tensor", lambda e, yb=yb, t0=t0, dh=dh: e.matmul(
                            yb[:, 0:512], C_km[0:16, t0:t0 + 128],
                            be2b[0:16, dh * 512:(dh + 1) * 512],
                            start=True, stop=False), waits=w)
                        b.emit("tensor", lambda e, yb=yb, dh=dh: e.matmul(
                            yb[:, 0:512], onesbf[0:1, 0:128],
                            bs2row[0:1, dh * 512:(dh + 1) * 512],
                            start=False, stop=True), incs=[(YD, 1)])
                nyd = b.now(YD)
                for tc in range(2):
                    for dh in range(2):
                        yb = pb[tc * 2 + dh]
                        xt = x_sb[:, (tg * 2 + tc) * D + dh * 512:
                                  (tg * 2 + tc) * D + (dh + 1) * 512]
                        b.emit("vector", lambda e, yb=yb, xt=xt: e.tensor_tensor(
                            xt, yb[:, 0:512], xt, op=ALU.add),
                            waits=[(YD, nyd - 3 + tc * 2 + dh)], incs=[(YF, 1)])

        def attn_phase():
            nyf = b.now(YF)
            qkd0 = b.now(QKD)
            b.emit("gpsimd", lambda e: e.dma_start(
                WT(0), wqT.rearrange("(c p) o -> p c o", p=128)),
                waits=[(YF, nyf)], incs=[(WL[0], 16)])
            b.emit("gpsimd", lambda e: e.dma_start(
                WT(1), wkT.rearrange("(c p) o -> p c o", p=128)), incs=[(WL[1], 16)])
            wld_for = {0: (WL[0], b.now(WL[0])), 1: (WL[1], b.now(WL[1]))}
            qkl_hist = {}
            scratch = [qs_d, ks_d]
            for mi in range(3):
                if mi == 2:
                    b.emit("gpsimd", lambda e: e.dma_start(
                        WT(0), wvT.rearrange("(c p) o -> p c o", p=128)),
                        waits=[(QKD, qkd0 + 32)], incs=[(WL[0], 16)])
                    wld_for[2] = (WL[0], b.now(WL[0]))
                for ti in range(NT):
                    bk0 = pb[(ti % 2) * 2]
                    bk1 = pb[(ti % 2) * 2 + 1]
                    for oh in range(2):
                        bk = bk0 if oh == 0 else bk1
                        for dc in range(NDC):
                            w = []
                            if oh == 0 and dc == 0:
                                w = [(TS[ti // 4], b.now(TS[ti // 4]))]
                                if ti == 0:
                                    w += [wld_for[mi], (YF, nyf),
                                          (XAT, b.now(XAT))]
                                key2 = ((mi, ti - 2) if ti >= 2 else
                                        ((mi - 1, NT - 2 + ti) if mi >= 1 else None))
                                if key2 is not None and key2 in qkl_hist:
                                    w.append((QKL, qkl_hist[key2]))
                            b.emit("tensor", lambda e, bk=bk, mi=mi, dc=dc, ti=ti, oh=oh: e.matmul(
                                bk[:, 0:512], xn_fm[:, dc, ti * 128:(ti + 1) * 128],
                                WT(mi % 2)[:, dc, oh * 512:(oh + 1) * 512],
                                start=(dc == 0), stop=(dc == NDC - 1)),
                                waits=w,
                                incs=[(QKD, 1)] if dc == NDC - 1 else [])
                    nqkd = b.now(QKD)
                    if mi < 2:
                        gb_ = gqb if mi == 0 else gkb
                        bb_ = bqb if mi == 0 else bkb
                        pv0 = bk0[:, 0:512].rearrange("p (s d) -> p s d", s=8)
                        pv1 = bk1[:, 0:512].rearrange("p (s d) -> p s d", s=8)
                        b.emit("vector", lambda e, pv0=pv0: e.reduce_sum(
                            qs1[:, 0:8], pv0, axis=AX),
                            waits=[(QKD, nqkd)], incs=[(QKL, 1)])
                        b.emit("vector", lambda e, pv1=pv1: e.reduce_sum(
                            qs1[:, 8:16], pv1, axis=AX), incs=[(QKL, 1)])
                        sq0 = scr32[:, 0:512]
                        sq1 = scr32[:, 512:1024]
                        b.emit("scalar", lambda e, bk0=bk0, sq0=sq0: e.activation(
                            sq0, bk0[:, 0:512], AF.Square),
                            waits=[(QKD, nqkd), (QKL, b.now(QKL))], incs=[(QKL, 1)])
                        b.emit("scalar", lambda e, bk1=bk1, sq1=sq1: e.activation(
                            sq1, bk1[:, 0:512], AF.Square), incs=[(QKL, 1)])
                        nq2 = b.now(QKL)
                        b.emit("vector", lambda e, sq0=sq0: e.reduce_sum(
                            qs2[:, 0:8], sq0.rearrange("p (s d) -> p s d", s=8), axis=AX),
                            waits=[(QKL, nq2)], incs=[(QKL, 1)])
                        b.emit("vector", lambda e, sq1=sq1: e.reduce_sum(
                            qs2[:, 8:16], sq1.rearrange("p (s d) -> p s d", s=8), axis=AX),
                            incs=[(QKL, 1)])
                        b.emit("vector", lambda e: e.tensor_scalar(
                            qm[:, 0:16], qs1[:, 0:16], 1.0 / DH, None, op0=ALU.mult),
                            incs=[(QKL, 1)])
                        b.emit("vector", lambda e: e.tensor_scalar(
                            qr[:, 0:16], qs2[:, 0:16], 1.0 / DH, None, op0=ALU.mult),
                            incs=[(QKL, 1)])
                        b.emit("vector", lambda e: e.tensor_tensor(
                            qmsq[:, 0:16], qm[:, 0:16], qm[:, 0:16], op=ALU.mult),
                            incs=[(QKL, 1)])
                        b.emit("vector", lambda e: e.tensor_sub(
                            qr[:, 0:16], qr[:, 0:16], qmsq[:, 0:16]), incs=[(QKL, 1)])
                        b.emit("vector", lambda e: e.tensor_scalar(
                            qr[:, 0:16], qr[:, 0:16], EPS, None, op0=ALU.add),
                            incs=[(QKL, 1)])
                        b.emit("scalar", lambda e: e.activation(
                            qr[:, 0:16], qr[:, 0:16], AF.Sqrt),
                            waits=[(QKL, b.now(QKL))], incs=[(QKL, 1)])
                        b.emit("vector", lambda e: e.reciprocal(
                            qr[:, 0:16], qr[:, 0:16]),
                            waits=[(QKL, b.now(QKL))], incs=[(QKL, 1)])
                        t1 = scr32[:, 0:1024]
                        for s in range(16):
                            bk = bk0 if s < 8 else bk1
                            so = (s % 8) * 64
                            b.emit("vector", lambda e, bk=bk, so=so, s=s, t1=t1: e.tensor_scalar(
                                t1[:, s * 64:(s + 1) * 64], bk[:, so:so + 64],
                                qm[:, s:s + 1], qr[:, s:s + 1],
                                op0=ALU.subtract, op1=ALU.mult),
                                incs=[(QKL, 1)] if s == 15 else [])
                        gprev = TS[((ti - 2) % NT) // 4]
                        wtm = [(gprev, b.now(gprev))]
                        b.emit("vector", lambda e, t1=t1, gb_=gb_: e.tensor_mul(
                            t1.rearrange("p (s d) -> p s d", s=16),
                            t1.rearrange("p (s d) -> p s d", s=16),
                            gb_[:].rearrange("p (o d) -> p o d", o=1).broadcast_to([128, 16, DH])),
                            incs=[(QKL, 1)])
                        tmt = tmr[:, ti % 2, :]
                        b.emit("vector", lambda e, t1=t1, tmt=tmt, bb_=bb_: e.tensor_add(
                            tmt.rearrange("p (s d) -> p s d", s=16),
                            t1.rearrange("p (s d) -> p s d", s=16),
                            bb_[:].rearrange("p (o d) -> p o d", o=1).broadcast_to([128, 16, DH])),
                            waits=wtm, incs=[(QKL, 1)])
                        nql = b.now(QKL)
                        p = ti % 2
                        g = TS[ti // 4]
                        fms = fmstg[p]
                        for dc in range(NDC):
                            b.emit("sync", lambda e, tmt=tmt, dc=dc, fms=fms: e.dma_start_transpose(
                                fms[:, dc * 128:(dc + 1) * 128],
                                tmt[:, dc * 128:(dc + 1) * 128]),
                                waits=([(QKL, nql), (SP[p], b.now(SP[p]))]
                                       if dc == 0 else []),
                                incs=[(g, 16)])
                        dst = scratch[mi]
                        b.emit("sync", lambda e, dst=dst, ti=ti, fms=fms: e.dma_start(
                            dst.rearrange("(c o) t -> o c t", o=128)[:, :, ti * 128:(ti + 1) * 128],
                            fms[:].rearrange("p (c t) -> p c t", c=8)),
                            waits=[(g, b.now(g))], incs=[(SP[p], 16)])
                        qkl_hist[(mi, ti)] = b.now(QKL)
                    else:
                        w = [(QKD, nqkd), (VSP, b.now(VSP))]
                        b.emit("vector", lambda e, bk0=bk0: e.tensor_copy(
                            vstg[:, 0:512], bk0[:, 0:512]),
                            waits=w, incs=[(QKL, 1)])
                        b.emit("vector", lambda e, bk1=bk1: e.tensor_copy(
                            vstg[:, 512:1024], bk1[:, 0:512]),
                            incs=[(QKL, 1)])
                        b.emit("sync", lambda e, ti=ti: e.dma_start(
                            vs_d[ti * 128:(ti + 1) * 128, :], vstg[:, 0:1024]),
                            waits=[(QKL, b.now(QKL))], incs=[(VSP, 16)])
                        qkl_hist[(2, ti)] = b.now(QKL)
            b.emit("gpsimd", lambda e: e.dma_start(
                WT(1), woT.rearrange("(c p) o -> p c o", p=128)),
                waits=[(QKD, qkd0 + 96)], incs=[(WL[1], 16)])
            wld_wo = (WL[1], b.now(WL[1]))

            n_spa = b.now(SP[0]); n_spb = b.now(SP[1]); n_vsp = b.now(VSP)

            avd_hist = {}
            hpl_hist = {}
            one_hist = {}
            exd_bank = {}
            pfr_slot = {}
            rcd_o = 0

            def prefetch_pair(hp):
                slot = hp % 2
                w = [(SP[0], n_spa), (SP[1], n_spb), (VSP, n_vsp)]
                if hp >= 2:
                    w.append((AVD, avd_hist[hp - 2]))
                b.emit("sync", lambda e, hp=hp, slot=slot: e.dma_start(
                    QPair(slot), qs_d[hp * 128:(hp + 1) * 128, :]),
                    waits=w, incs=[(HP[slot], 16)])
                b.emit("sync", lambda e, hp=hp, slot=slot: e.dma_start(
                    KPair(slot), ks_d[hp * 128:(hp + 1) * 128, :]),
                    incs=[(HP[slot], 16)])
                b.emit("sync", lambda e, hp=hp, slot=slot: e.dma_start(
                    VPair(slot)[:, :, 0:64],
                    vs_d.rearrange("(c p) d -> p c d", p=128)[:, :, hp * 128: hp * 128 + 64]),
                    incs=[(HP[slot], 16)])
                b.emit("sync", lambda e, hp=hp, slot=slot: e.dma_start(
                    VPair(slot)[:, :, 65:129],
                    vs_d.rearrange("(c p) d -> p c d", p=128)[:, :, hp * 128 + 64: hp * 128 + 128]),
                    incs=[(HP[slot], 16)])
                hpl_hist[hp] = (HP[slot], b.now(HP[slot]))
                b.emit("vector", lambda e, slot=slot: e.memset(
                    VPair(slot)[:, :, 64:65], 1.0),
                    waits=[hpl_hist[hp]], incs=[(SMK, 1)])
                b.emit("vector", lambda e, slot=slot: e.memset(
                    VPair(slot)[:, :, 129:130], 1.0), incs=[(SMK, 1)])
                one_hist[hp] = b.now(SMK)

            prefetch_pair(0)
            for hp in range(NH // 2):
                slot = hp % 2
                if hp + 1 < NH // 2:
                    prefetch_pair(hp + 1)
                for hh in range(2):
                    hof = hh * 64
                    for q in range(NT):
                        W_ = (q + 1) * 128
                        nchunks = (W_ + 511) // 512
                        for j in range(nchunks):
                            wdt = min(512, W_ - j * 512)
                            wv = []
                            if j == 0:
                                wv = [(SMK, b.now(SMK))]
                                if q == 0:
                                    wv.append(hpl_hist[hp])
                                    wv.append((SMK, one_hist[hp]))
                            b.emit("tensor", lambda e, j=j, q=q, wdt=wdt, slot=slot, hof=hof: e.matmul(
                                pb[j][:, 0:wdt],
                                QPair(slot)[hof:hof + 64, q * 128:(q + 1) * 128],
                                KPair(slot)[hof:hof + 64, j * 512:j * 512 + wdt],
                                start=True, stop=True),
                                waits=wv,
                                incs=[(STM, 1)] if j == nchunks - 1 else [])
                        nstm = b.now(STM)
                        jd = q // 4
                        cof = (q % 4) * 128
                        b.emit("vector", lambda e, jd=jd, cof=cof: e.tensor_tensor(
                            pb[jd][:, cof:cof + 128], pb[jd][:, cof:cof + 128],
                            mtm[:], op=ALU.add),
                            waits=[(STM, nstm)], incs=[(SMK, 1)])
                        for j in range(nchunks):
                            wdt = min(512, W_ - j * 512)
                            b.emit("vector", lambda e, j=j, wdt=wdt: e.reduce_max(
                                tmp4[:, j:j + 1], pb[j][:, 0:wdt], axis=AX),
                                incs=[(SMK, 1)])
                        b.emit("vector", lambda e, q=q, nchunks=nchunks: e.tensor_reduce(
                            nmc[:, q:q + 1], tmp4[:, 0:nchunks], axis=AX,
                            op=ALU.max, negate=True),
                            incs=[(SMK, 1)])
                    nsm = b.now(SMK)
                    b.emit("tensor", lambda e: e.transpose(
                        pb[7][0:16, 0:128], nmc[:, 0:16], idn[:]),
                        waits=[(SMK, nsm), (RCD, rcd_o)], incs=[(NMT, 1)])
                    b.emit("vector", lambda e: e.tensor_copy(
                        nmT_sb[0:16, 0:128], pb[7][0:16, 0:128]),
                        waits=[(NMT, b.now(NMT))], incs=[(NMT, 1)])
                    b.emit("sync", lambda e: e.dma_start(
                        nmrow[0:1, :].rearrange("o (c t) -> o c t", c=16),
                        nmT_sb[0:16, 0:128]),
                        waits=[(NMT, b.now(NMT))], incs=[(NMT, 16)])
                    n_nmrow = b.now(NMT)

                    tiles = [(g, c) for g in range(4) for c in range(4 * (g + 1))]

                    def emit_preload_sp(g, c):
                        sb_ = pb[4 + (c % 2)]
                        w = [(NMT, n_nmrow)] if (g == 0 and c == 0) else []
                        if (c % 2) in exd_bank:
                            w.append((EXD, exd_bank[(c % 2)]))
                        b.emit("tensor", lambda e, sb_=sb_, g=g: e.matmul(
                            sb_[:, 0:512], onesbf[0:1, 0:128],
                            nmrow[0:1, g * 512:(g + 1) * 512],
                            start=True, stop=False), waits=w)
                        b.emit("tensor", lambda e, sb_=sb_, c=c, g=g, slot=slot, hof=hof: e.matmul(
                            sb_[:, 0:512],
                            KPair(slot)[hof:hof + 64, c * 128:(c + 1) * 128],
                            QPair(slot)[hof:hof + 64, g * 512:(g + 1) * 512],
                            start=False, stop=True), incs=[(SKM, 1)])

                    def emit_expav(g, c, last_in_group):
                        sb_ = pb[4 + (c % 2)]
                        pslot = c % 4
                        nskm = skm_of[(g, c)]
                        r = c - 4 * g
                        wexp = [(SKM, nskm)]
                        if r >= 0:
                            b.emit("vector", lambda e, sb_=sb_, r=r: e.tensor_tensor(
                                sb_[:, r * 128:(r + 1) * 128],
                                sb_[:, r * 128:(r + 1) * 128], mut[:], op=ALU.add),
                                waits=[(SKM, nskm)], incs=[(KMM, 1)])
                            wexp.append((KMM, b.now(KMM)))
                        if pslot in pfr_slot:
                            wexp.append((AVD, pfr_slot[pslot]))
                        b.emit("scalar", lambda e, sb_=sb_, pslot=pslot: e.activation(
                            PP(pslot), sb_[:, 0:512], AF.Exp),
                            waits=wexp, incs=[(EXD, 1)])
                        if r > 0:
                            b.emit("scalar", lambda e, pslot=pslot, r=r: e.memzero(
                                PP(pslot)[:, 0:r * 128]), incs=[(EXD, 1)])
                        exd_bank[(c % 2)] = b.now(EXD)
                        exd_of[(g, c)] = b.now(EXD)

                    def emit_av(g, c, nch):
                        pslot = c % 4
                        w = [(EXD, exd_of[(g, c)])]
                        if c == 0:
                            w.append((RCD, rcd_o))
                            w.append((SMK, one_hist[hp]))
                        b.emit("tensor", lambda e, slot=slot, hh=hh, c=c, pslot=pslot, nch=nch: e.matmul(
                            pb[6][0:65, 0:512],
                            VPair(slot)[:, c, hh * 65:(hh + 1) * 65],
                            PP(pslot),
                            start=(c == 0), stop=(c == nch - 1)),
                            waits=w,
                            incs=[(AVD, 1)])
                        pfr_slot[pslot] = b.now(AVD)

                    def emit_group_end(g):
                        nonlocal rcd_o
                        navd = b.now(AVD)
                        b.emit("vector", lambda e: e.reciprocal(
                            rrow32[0:1, 0:512], pb[6][64:65, 0:512]),
                            waits=[(AVD, navd)], incs=[(RCD, 1)])
                        b.emit("vector", lambda e: e.tensor_copy(
                            rrow16[0:1, 0:512], rrow32[0:1, 0:512]),
                            incs=[(RCD, 1)])
                        b.emit("tensor", lambda e: e.matmul(
                            pb[7][0:64, 0:512], onesbf[0:1, 0:64], rrow16[0:1, 0:512],
                            start=True, stop=True),
                            waits=[(RCD, b.now(RCD))], incs=[(RCD, 1)])
                        b.emit("vector", lambda e: e.tensor_copy(
                            rb32[0:64, 0:512], pb[7][0:64, 0:512]),
                            waits=[(RCD, b.now(RCD))], incs=[(RCD, 1)])
                        b.emit("vector", lambda e, hp=hp, hof=hof, g=g: e.tensor_mul(
                            xn_fm[hof:hof + 64, hp, g * 512:(g + 1) * 512],
                            pb[6][0:64, 0:512], rb32[0:64, 0:512]),
                            waits=[(RCD, b.now(RCD))], incs=[(RCD, 1)])
                        rcd_o = b.now(RCD)

                    skm_of = {}
                    exd_of = {}
                    prev = None
                    for idx, (g, c) in enumerate(tiles):
                        emit_preload_sp(g, c)
                        skm_of[(g, c)] = b.now(SKM)
                        if prev is not None:
                            pg, pc = prev
                            nch_p = 4 * (pg + 1)
                            emit_expav(pg, pc, pc == nch_p - 1)
                            emit_av(pg, pc, nch_p)
                            if pc == nch_p - 1:
                                emit_group_end(pg)
                        prev = (g, c)

                    pg, pc = prev
                    nch_p = 4 * (pg + 1)
                    emit_expav(pg, pc, True)
                    emit_av(pg, pc, nch_p)
                    emit_group_end(pg)
                avd_hist[hp] = b.now(AVD)

            n_rcd = b.now(RCD)
            xat_hist = {}
            for ti in range(NT):
                bk0 = pb[(ti % 2) * 2]
                bk1 = pb[(ti % 2) * 2 + 1]
                for oh in range(2):
                    bk = bk0 if oh == 0 else bk1
                    for dc in range(NDC):
                        w = []
                        if oh == 0 and dc == 0:
                            if ti == 0:
                                w = [wld_wo, (RCD, n_rcd)]
                            if ti >= 2:
                                w.append((XAT, xat_hist[ti - 2]))
                        b.emit("tensor", lambda e, bk=bk, dc=dc, ti=ti, oh=oh: e.matmul(
                            bk[:, 0:512], xn_fm[:, dc, ti * 128:(ti + 1) * 128],
                            WT(1)[:, dc, oh * 512:(oh + 1) * 512],
                            start=(dc == 0), stop=False),
                            waits=w)
                    b.emit("tensor", lambda e, bk=bk, oh=oh: e.matmul(
                        bk[:, 0:512], onesbf[0:1, 0:128],
                        borow[0:1, oh * 512:(oh + 1) * 512],
                        start=False, stop=True), incs=[(QKD, 1)])
                nq = b.now(QKD)
                for oh in range(2):
                    bk = bk0 if oh == 0 else bk1
                    xt = x_sb[:, ti * D + oh * 512: ti * D + (oh + 1) * 512]
                    b.emit("vector", lambda e, bk=bk, xt=xt: e.tensor_tensor(
                        xt, bk[:, 0:512], xt, op=ALU.add),
                        waits=[(QKD, nq - 1 + oh)], incs=[(XAT, 1)])
                xat_hist[ti] = b.now(XAT)

        ln_phase()
        if STOP_AFTER == "ln":
            pass
        elif isinstance(STOP_AFTER, str) and STOP_AFTER.startswith("u"):
            ffn_phase(nunits=int(STOP_AFTER[1:]), bias=False)
        elif isinstance(STOP_AFTER, str) and STOP_AFTER.startswith("b"):
            ffn_phase(nunits=int(STOP_AFTER[1:]), bias=True)
        else:
            ffn_phase()
            if STOP_AFTER != "ffn1":
                ln_phase()
                attn_phase()
                if STOP_AFTER != "attn":
                    ln_phase()
                    ffn_phase()
        b.emit("sync", lambda e: e.dma_start(
            out.rearrange("(t p) d -> p t d", p=128),
            x_sb[:].rearrange("p (t d) -> p t d", t=NT)),
            waits=[(YF, b.now(YF)), (XAT, b.now(XAT))], incs=[(ST, 16)])
        if DEBUG_DUMP:
            dxn = nc.dram_tensor("dbg_xn", [128, NDC * L], BF16,
                                 kind="ExternalOutput").ap()
            du = nc.dram_tensor("dbg_u16", [128, 37120], BF16,
                                kind="ExternalOutput").ap()
            dg = nc.dram_tensor("dbg_gsc", [128, 1792], F32,
                                kind="ExternalOutput").ap()
            dt_ = nc.dram_tensor("dbg_tmr", [128, 2 * D], BF16,
                                 kind="ExternalOutput").ap()
            ds = nc.dram_tensor("dbg_stat", [128, 192], F32,
                                kind="ExternalOutput").ap()
            wdump = [(t_, b.now(t_)) for t_ in TS] + [
                (LNP, b.now(LNP)), (CBE, b.now(CBE)), (GPP, b.now(GPP)),
                (HSF, b.now(HSF))]
            b.emit("sync", lambda e: e.dma_start(
                dxn.rearrange("p (c t) -> p c t", c=NDC), xn_fm[:]),
                waits=wdump, incs=[(ST, 16)])
            b.emit("sync", lambda e: e.dma_start(du, u16[:]), incs=[(ST, 16)])
            b.emit("sync", lambda e: e.dma_start(dg, gsc[:]), incs=[(ST, 16)])
            b.emit("sync", lambda e: e.dma_start(
                dt_.rearrange("p (s d) -> p s d", s=2), tmr[:]), incs=[(ST, 16)])
            b.emit("sync", lambda e: e.dma_start(ds, stat32[:]), incs=[(ST, 16)])
            dsc = nc.dram_tensor("dbg_scr", [128, 1024], F32,
                                 kind="ExternalOutput").ap()
            b.emit("sync", lambda e: e.dma_start(dsc, scr32[:]), incs=[(ST, 16)])
        b.emit("sync", lambda e: None, waits=[(ST, b.now(ST))])

        import contextlib
        with contextlib.ExitStack() as stack:
            chained = ("vector", "scalar") if not DEBUG_SERIAL else (
                "vector", "scalar", "tensor")
            for nm in chained:
                b.sem("priv_" + nm)
            for nm in b.semnames:
                b.sems[nm] = stack.enter_context(nc.semaphore(nm))

            streams = {"sync": [], "tensor": [], "vector": [], "scalar": [],
                       "gpsimd": []}
            for op in b.ops:
                streams[op.eng].append(op)

            def runner(ops, eng_name):
                priv = (b.sems.get("priv_" + eng_name)
                        if eng_name in chained else None)

                def run(eng):
                    cnt = 0
                    pend = []
                    for op in ops:
                        for s, v in op.waits:
                            eng.wait_ge(b.sems[s], v)
                        if priv is not None and not op.dma:
                            for s, v in pend:
                                eng.wait_ge(b.sems[s] if s != "__priv" else priv, v)
                            pend = []
                        inst = op.fn(eng)
                        if inst is None:
                            continue
                        if priv is not None and not op.dma:
                            if op.incs:
                                pend = list(op.incs_after)
                            else:
                                cnt += 1
                                inst.then_inc(priv, 1)
                                pend = [("__priv", cnt)]
                        for s, n in op.incs:
                            inst.then_inc(b.sems[s], n)
                return run

            blk.sync(runner(streams["sync"], "sync"))
            blk.tensor(runner(streams["tensor"], "tensor"))
            blk.vector(runner(streams["vector"], "vector"))
            blk.scalar(runner(streams["scalar"], "scalar"))
            blk.gpsimd(runner(streams["gpsimd"], "gpsimd"))

    return nc


_prog = None


def kernel(**inputs):
    global _prog
    f32 = np.float32

    def cc(a):
        return np.ascontiguousarray(np.asarray(a, dtype=f32))

    x = cc(inputs["L0_attn_x"])
    base = {
        "wqT": cc(np.asarray(inputs["L0_attn_Wq"], dtype=f32).T),
        "wkT": cc(np.asarray(inputs["L0_attn_Wk"], dtype=f32).T),
        "wvT": cc(np.asarray(inputs["L0_attn_Wv"], dtype=f32).T),
        "woT": cc(np.asarray(inputs["L0_attn_Wo"], dtype=f32).T),
        "bo": cc(inputs["L0_attn_bo"]).reshape(1, D),
        "gq_b": cc(np.tile(np.asarray(inputs["L0_attn_gq"], dtype=f32).reshape(1, DH), (128, 1))),
        "bq_b": cc(np.tile(np.asarray(inputs["L0_attn_bq"], dtype=f32).reshape(1, DH), (128, 1))),
        "gk_b": cc(np.tile(np.asarray(inputs["L0_attn_gk"], dtype=f32).reshape(1, DH), (128, 1))),
        "bk_b": cc(np.tile(np.asarray(inputs["L0_attn_bk"], dtype=f32).reshape(1, DH), (128, 1))),
        "gwT": cc(np.asarray(inputs["L0_ffn_gate_w"], dtype=f32).T),
        "gb_b": cc(np.tile(np.asarray(inputs["L0_ffn_gate_b"], dtype=f32).reshape(1, NE), (128, 1))),
        "we1": cc(inputs["L0_ffn_We1"]),
        "we3": cc(inputs["L0_ffn_We3"]),
        "we2": cc(inputs["L0_ffn_We2"]),
        "ws1": cc(inputs["L0_ffn_ws1"]),
        "ws3": cc(inputs["L0_ffn_ws3"]),
        "ws2": cc(inputs["L0_ffn_ws2"]),
        "becol1": cc(np.transpose(
            np.asarray(inputs["L0_ffn_be1"], dtype=f32).T.reshape(4, 128, NE),
            (1, 2, 0))),
        "becol3": cc(np.transpose(
            np.asarray(inputs["L0_ffn_be3"], dtype=f32).T.reshape(4, 128, NE),
            (1, 2, 0))),
        "bscol1": cc(np.asarray(inputs["L0_ffn_bs1"], dtype=f32).reshape(8, 128).T),
        "bscol3": cc(np.asarray(inputs["L0_ffn_bs3"], dtype=f32).reshape(8, 128).T),
        "be2": cc(inputs["L0_ffn_be2"]),
        "bs2": cc(inputs["L0_ffn_bs2"]).reshape(1, D),
        "oneh": cc(np.kron(np.eye(NE), np.ones((1, 128)))),
        "mask_tm": cc(np.where(
            np.arange(128)[None, :] <= np.arange(128)[:, None], 0.0, NEG)),
        "mask_ut": cc(np.where(
            np.arange(128)[None, :] >= np.arange(128)[:, None], 0.0, NEG)),
        "ident": cc(np.eye(128)),
    }
    if _prog is None:
        _prog = build_program()
    in_maps = [dict(base, x=np.ascontiguousarray(x[c])) for c in range(NCORE)]
    res = run_bass_kernel_spmd(_prog, in_maps, core_ids=list(range(NCORE)))
    global _last_results
    _last_results = res.results
    return np.stack([r["out"] for r in res.results], axis=0)


# version 07 (final)
# speedup vs baseline: 1.0822x; 1.0822x over previous
import numpy as np
import concourse.bass as bass
import concourse.mybir as mybir
from concourse.bass_utils import run_bass_kernel_spmd

F32 = mybir.dt.float32
BF16 = mybir.dt.bfloat16
AF = mybir.ActivationFunctionType
ALU = mybir.AluOpType
AX = mybir.AxisListType.X

NCORE = 8
L = 2048
D = 1024
NT = L // 128
NDC = D // 128
NE = 16
NU = 18
TG = 256
NTG = L // TG
NH = 16
DH = 64
NEG = -1e30
EPS = 1e-5
STOP_AFTER = None
DEBUG_SERIAL = False
DEBUG_DUMP = False


class Op:
    __slots__ = ("eng", "fn", "waits", "incs", "incs_after", "chain")

    def __init__(self, eng, fn, waits, incs, incs_after, chain=True):
        self.eng = eng
        self.fn = fn
        self.waits = waits
        self.incs = incs
        self.incs_after = incs_after
        self.chain = chain


class B:
    def __init__(self):
        self.ops = []
        self.cnt = {}
        self.sems = {}
        self.semnames = []

    def sem(self, name):
        if name not in self.cnt:
            self.cnt[name] = 0
            self.semnames.append(name)
        return name

    def emit(self, eng, fn, waits=(), incs=(), chain=True):
        for s, n in incs:
            self.sem(s)
        ws = [(s, v) for (s, v) in waits if v > 0]
        for s, v in ws:
            self.sem(s)
        op = Op(eng, fn, ws, list(incs), [], chain)
        self.ops.append(op)
        for s, n in incs:
            self.cnt[s] += n
            op.incs_after.append((s, self.cnt[s]))

    def now(self, s):
        return self.cnt.get(s, 0)


def build_program():
    nc = bass.Bass("TRN2", target_bir_lowering=False, detect_race_conditions=False)

    def din(name, shape, dtype=F32):
        return nc.dram_tensor(name, shape, dtype, kind="ExternalInput").ap()

    x_in = din("x", [L, D])
    wqT = din("wqT", [D, D]); wkT = din("wkT", [D, D])
    wvT = din("wvT", [D, D]); woT = din("woT", [D, D])
    bo_in = din("bo", [1, D])
    gq_in = din("gq_b", [128, DH]); bq_in = din("bq_b", [128, DH])
    gk_in = din("gk_b", [128, DH]); bk_in = din("bk_b", [128, DH])
    gwT = din("gwT", [D, NE])
    gb_in = din("gb_b", [128, NE])
    we1 = din("we1", [NE, D, 512]); we3 = din("we3", [NE, D, 512])
    we2 = din("we2", [NE, 512, D])
    ws1 = din("ws1", [D, D]); ws3 = din("ws3", [D, D]); ws2 = din("ws2", [D, D])
    becol1 = din("becol1", [128, NE, 4]); becol3 = din("becol3", [128, NE, 4])
    bscol1 = din("bscol1", [128, 8]); bscol3 = din("bscol3", [128, 8])
    be2_in = din("be2", [NE, D])
    bs2_in = din("bs2", [1, D])
    oneh_in = din("oneh", [NE, L])
    mask_tm_in = din("mask_tm", [128, 128])
    mask_ut_in = din("mask_ut", [128, 128])
    ident_in = din("ident", [128, 128])
    out = nc.dram_tensor("out", [L, D], F32, kind="ExternalOutput").ap()

    qs_d = nc.dram_tensor("qs_d", [D, L], BF16, kind="Internal").ap()
    ks_d = nc.dram_tensor("ks_d", [D, L], BF16, kind="Internal").ap()
    vs_d = nc.dram_tensor("vs_d", [L, D], BF16, kind="Internal").ap()

    import contextlib
    _stk = contextlib.ExitStack()
    with _stk:
        x_sb = _stk.enter_context(nc.sbuf_tensor("x_sb", [128, NT * D], F32))
        stage32 = _stk.enter_context(nc.sbuf_tensor("stage32", [128, 2048], F32))
        scr32 = _stk.enter_context(nc.sbuf_tensor("scr32", [128, 1024], F32))
        stat32 = _stk.enter_context(nc.sbuf_tensor("stat32", [128, 192], F32))
        gqb = _stk.enter_context(nc.sbuf_tensor("gqb", [128, DH], F32))
        bqb = _stk.enter_context(nc.sbuf_tensor("bqb", [128, DH], F32))
        gkb = _stk.enter_context(nc.sbuf_tensor("gkb", [128, DH], F32))
        bkb = _stk.enter_context(nc.sbuf_tensor("bkb", [128, DH], F32))
        gbb = _stk.enter_context(nc.sbuf_tensor("gbb", [128, NE], F32))
        bec1 = _stk.enter_context(nc.sbuf_tensor("bec1", [128, NE, 4], F32))
        bec3 = _stk.enter_context(nc.sbuf_tensor("bec3", [128, NE, 4], F32))
        bsc1 = _stk.enter_context(nc.sbuf_tensor("bsc1", [128, 8], F32))
        bsc3 = _stk.enter_context(nc.sbuf_tensor("bsc3", [128, 8], F32))
        mtm = _stk.enter_context(nc.sbuf_tensor("mtm", [128, 128], F32))
        mut = _stk.enter_context(nc.sbuf_tensor("mut", [128, 128], F32))
        idn = _stk.enter_context(nc.sbuf_tensor("idn", [128, 128], F32))
        gsc = _stk.enter_context(nc.sbuf_tensor("gsc", [128, 1792], F32))
        xn_fm = _stk.enter_context(nc.sbuf_tensor("xn_fm", [128, NDC, L], BF16))
        tmr = _stk.enter_context(nc.sbuf_tensor("tmr", [128, 2, D], BF16))
        u16 = _stk.enter_context(nc.sbuf_tensor("u16", [128, 37120], BF16))
        be2b = _stk.enter_context(nc.sbuf_tensor("be2b", [NE, D], BF16))
        bs2row = _stk.enter_context(nc.sbuf_tensor("bs2row", [1, D], BF16))
        borow = _stk.enter_context(nc.sbuf_tensor("borow", [1, D], BF16))
        onesbf = _stk.enter_context(nc.sbuf_tensor("onesbf", [1, 128], BF16))
        gwTs = _stk.enter_context(nc.sbuf_tensor("gwTs", [128, NDC, NE], BF16))
        silu_r = _stk.enter_context(nc.sbuf_tensor("silu_r", [128, 2, TG], BF16))
        h3s_r = _stk.enter_context(nc.sbuf_tensor("h3s_r", [128, 2, TG], BF16))
        p0 = _stk.enter_context(nc.psum_tensor("p0", [128, 512], F32))
        p1 = _stk.enter_context(nc.psum_tensor("p1", [128, 512], F32))
        p2 = _stk.enter_context(nc.psum_tensor("p2", [128, 512], F32))
        p3 = _stk.enter_context(nc.psum_tensor("p3", [128, 512], F32))
        p4 = _stk.enter_context(nc.psum_tensor("p4", [128, 512], F32))
        p5 = _stk.enter_context(nc.psum_tensor("p5", [128, 512], F32))
        p6 = _stk.enter_context(nc.psum_tensor("p6", [128, 512], F32))
        p7 = _stk.enter_context(nc.psum_tensor("p7", [128, 512], F32))
        blk = _stk.enter_context(nc.Block())
        rb32 = scr32[:, 0:512]
        vstg = stage32[:, 0:512].bitcast(BF16)
        fmstg = [stage32[:, 512:1024].bitcast(BF16),
                 stage32[:, 1280:1792].bitcast(BF16)]
        rrow16 = stage32[0:1, 1024:1280].bitcast(BF16)
        rrow32 = stage32[0:1, 1280:1792]
        pb = [p0, p1, p2, p3, p4, p5, p6, p7]
        b = B()

        def W1(s):
            return u16[:, s * 12288: s * 12288 + 4096].rearrange("p (c i) -> p c i", c=8)

        def W3(s):
            return u16[:, s * 12288 + 4096: s * 12288 + 8192].rearrange("p (c i) -> p c i", c=8)

        def W2(s):
            return u16[:, s * 12288 + 8192: s * 12288 + 12288].rearrange("p (c i) -> p c i", c=4)

        def HS(s):
            return u16[:, 24576 + s * 1024: 24576 + (s + 1) * 1024]

        def CBC(s):
            return u16[:, 26624 + s * 2048: 26624 + (s + 1) * 2048]

        C_km = u16[0:NE, 30720:32768]
        onehot = u16[0:NE, 32768:34816]

        def QPair(s):
            return u16[:, s * 2048: (s + 1) * 2048]

        def KPair(s):
            return u16[:, 4096 + s * 2048: 4096 + (s + 1) * 2048]

        def VPair(s):
            return u16[:, 8192 + s * 2080: 8192 + (s + 1) * 2080].rearrange(
                "p (c w) -> p c w", c=16)

        def PP(s):
            return u16[:, 12352 + s * 512: 12352 + (s + 1) * 512]

        def WT(s):
            return u16[:, 14400 + s * 8192: 14400 + (s + 1) * 8192].rearrange(
                "p (c o) -> p c o", c=8)

        nmrow = u16[0:1, 34816:36864]
        nmT_sb = u16[0:16, 36864:36992]

        s1c = stat32[:, 0:16]; s2c = stat32[:, 16:32]
        mc = stat32[:, 32:48]; rc = stat32[:, 48:64]
        qs1 = stat32[:, 64:80]; qs2 = stat32[:, 80:96]
        qm = stat32[:, 96:112]; qr = stat32[:, 112:128]
        nmc = stat32[:, 128:144]
        tmp4 = stat32[:, 144:148]
        lnt = stat32[:, 148:149]
        qmsq = stat32[:, 150:166]

        g_exp = gsc[:, 0:256]; g_orig = gsc[:, 256:512]
        g_sc2 = gsc[:, 512:768]; g_v2 = gsc[:, 768:1024]
        g_kn = gsc[:, 1024:1280]; g_Cf = gsc[:, 1280:1536]
        g_m1g = gsc[:, 1536:1600]; g_t = gsc[:, 1600:1664]
        g_gs = gsc[:, 1664:1728]; g_sm = gsc[:, 1728:1792]

        LD = "ld"; LDG = "ldg"
        WL = ["wla", "wlb"]
        TS = ["tsa", "tsb"]
        SP = ["spa", "spb"]
        VSP = "vsp"
        HP = ["hpa", "hpb"]
        HD = "hd"; AD = "ad"; HSF = "hsf"; YD = "yd"; YF = "yf"
        CBD = "cbd"; CBE = "cbe"; GMM = "gmm"; GPP = "gpp"
        SETE = "sete"; LNP = "lnp"
        QKD = "qkd"; QKL = "qkl"
        STM = "stm"; SMK = "smk"; NMT = "nmt"; SKM = "skm"; KMM = "kmm"
        EXD = "exd"; AVD = "avd"; RCD = "rcd"; XAT = "xat"
        ST = "st"
        for nm in (LD, LDG, WL[0], WL[1], TS[0], TS[1], SP[0], SP[1], VSP,
                   HP[0], HP[1], HD, AD, HSF, YD, YF, CBD, CBE, GMM, GPP,
                   SETE, LNP, QKD, QKL, STM, SMK, NMT, SKM,
                   KMM, EXD, AVD, RCD, XAT, ST):
            b.sem(nm)

        def ld_sync(dst_ap, src_ap, waits=()):
            b.emit("sync", lambda e, d=dst_ap, s=src_ap: e.dma_start(d, s),
                   waits=waits, incs=[(LD, 16)])

        def ld_gp(dst_ap, src_ap, waits=()):
            b.emit("gpsimd", lambda e, d=dst_ap, s=src_ap: e.dma_start(d, s),
                   waits=waits, incs=[(LDG, 16)])

        ld_sync(x_sb[:].rearrange("p (t d) -> p t d", t=NT),
                x_in.rearrange("(t p) d -> p t d", p=128))
        ld_sync(bec1[:], becol1); ld_sync(bec3[:], becol3)
        ld_sync(bsc1[:], bscol1); ld_sync(bsc3[:], bscol3)
        ld_sync(mtm[:], mask_tm_in); ld_sync(mut[:], mask_ut_in)
        ld_sync(idn[:], ident_in)
        ld_sync(gqb[:], gq_in); ld_sync(bqb[:], bq_in)
        ld_sync(gkb[:], gk_in); ld_sync(bkb[:], bk_in)
        ld_sync(gbb[:], gb_in)
        ld_gp(gwTs[:], gwT.rearrange("(c p) e -> p c e", p=128))
        ld_gp(be2b[:], be2_in)
        ld_gp(bs2row[:], bs2_in)
        ld_gp(borow[:], bo_in)
        ld_gp(onehot, oneh_in)
        n_ld = b.now(LD)
        n_ldg = b.now(LDG)

        b.emit("vector", lambda e: e.memset(onesbf[0:1, :], 1.0), incs=[(SETE, 1)])
        b.emit("vector", lambda e: e.tensor_scalar(
            be2b[:], be2b[:], 0.5, None, op0=ALU.mult),
            waits=[(LDG, n_ldg)], incs=[(SETE, 1)])
        b.emit("vector", lambda e: e.tensor_scalar(
            bs2row[:], bs2row[:], 0.5, None, op0=ALU.mult), incs=[(SETE, 1)])
        b.emit("vector", lambda e: e.tensor_scalar(
            bsc3[:], bsc3[:], 0.5, None, op0=ALU.mult),
            waits=[(LD, n_ld)], incs=[(SETE, 1)])

        def ln_phase():
            nyf = b.now(YF)
            nxat = b.now(XAT)
            for ti in range(NT):
                xt = x_sb[:, ti * D:(ti + 1) * D]
                sg = stage32[:, (ti % 2) * 1024:(ti % 2) * 1024 + 1024]
                tmt = tmr[:, ti % 2, :]
                b.emit("vector", lambda e, xt=xt, ti=ti: e.reduce_sum(
                    s1c[:, ti:ti + 1], xt, axis=AX),
                    waits=[(LD, n_ld), (YF, nyf), (XAT, nxat)], incs=[(LNP, 1)])
                b.emit("scalar", lambda e, xt=xt, sg=sg: e.activation(
                    sg, xt, AF.Square),
                    waits=[(LNP, b.now(LNP))],
                    incs=[(LNP, 1)], chain=False)
                b.emit("vector", lambda e, sg=sg, ti=ti: e.reduce_sum(
                    s2c[:, ti:ti + 1], sg, axis=AX),
                    waits=[(LNP, b.now(LNP))], incs=[(LNP, 1)])
                b.emit("vector", lambda e, ti=ti: e.tensor_scalar(
                    mc[:, ti:ti + 1], s1c[:, ti:ti + 1], 1.0 / D, None, op0=ALU.mult),
                    incs=[(LNP, 1)])
                b.emit("vector", lambda e, ti=ti: e.tensor_scalar(
                    rc[:, ti:ti + 1], s2c[:, ti:ti + 1], 1.0 / D, None, op0=ALU.mult),
                    incs=[(LNP, 1)])
                b.emit("vector", lambda e, ti=ti: e.tensor_tensor(
                    lnt[:, 0:1], mc[:, ti:ti + 1], mc[:, ti:ti + 1], op=ALU.mult),
                    incs=[(LNP, 1)])
                b.emit("vector", lambda e, ti=ti: e.tensor_sub(
                    rc[:, ti:ti + 1], rc[:, ti:ti + 1], lnt[:, 0:1]),
                    incs=[(LNP, 1)])
                b.emit("vector", lambda e, ti=ti: e.tensor_scalar(
                    rc[:, ti:ti + 1], rc[:, ti:ti + 1], EPS, None, op0=ALU.add),
                    incs=[(LNP, 1)])
                b.emit("scalar", lambda e, ti=ti: e.activation(
                    rc[:, ti:ti + 1], rc[:, ti:ti + 1], AF.Sqrt),
                    waits=[(LNP, b.now(LNP))], incs=[(LNP, 1)], chain=False)
                b.emit("vector", lambda e, ti=ti: e.reciprocal(
                    rc[:, ti:ti + 1], rc[:, ti:ti + 1]),
                    waits=[(LNP, b.now(LNP))], incs=[(LNP, 1)])
                p = TS[ti % 2]
                wtm = [(p, b.now(p))]
                b.emit("vector", lambda e, xt=xt, tmt=tmt, ti=ti: e.tensor_scalar(
                    tmt, xt, mc[:, ti:ti + 1], rc[:, ti:ti + 1],
                    op0=ALU.subtract, op1=ALU.mult),
                    waits=wtm, incs=[(LNP, 1)])
                nlnp = b.now(LNP)
                for dc in range(NDC):
                    b.emit("sync", lambda e, tmt=tmt, dc=dc, ti=ti: e.dma_start_transpose(
                        xn_fm[:, dc, ti * 128:(ti + 1) * 128],
                        tmt[:, dc * 128:(dc + 1) * 128]),
                        waits=[(LNP, nlnp)] if dc == 0 else [], incs=[(p, 16)])

        def gate_phase():
            n_tsa = b.now(TS[0]); n_tsb = b.now(TS[1])
            nyf = b.now(YF)
            nxat = b.now(XAT)
            nsete = b.now(SETE)
            for ti in range(NT):
                for dc in range(NDC):
                    b.emit("tensor", lambda e, ti=ti, dc=dc: e.matmul(
                        p0[:, ti * 16:(ti + 1) * 16],
                        xn_fm[:, dc, ti * 128:(ti + 1) * 128],
                        gwTs[:, dc, :],
                        start=(dc == 0), stop=(dc == NDC - 1)),
                        waits=([(TS[0], n_tsa), (TS[1], n_tsb), (LDG, n_ldg),
                                (YF, nyf), (XAT, nxat), (SETE, nsete)]
                               if (ti == 0 and dc == 0) else []),
                        incs=[(GMM, 1)] if (ti == NT - 1 and dc == NDC - 1) else [])
            b.emit("vector", lambda e: e.reduce_max(
                g_sm[:, 0:16], p0[:, 0:256].rearrange("p (t e) -> p t e", t=16),
                axis=AX),
                waits=[(GMM, b.now(GMM))], incs=[(GPP, 1)])
            if DEBUG_DUMP:
                b.emit("vector", lambda e: e.tensor_copy(
                    scr32[:, 0:256], p0[:, 0:256]), incs=[(GPP, 1)])
                b.emit("vector", lambda e: e.tensor_copy(
                    scr32[:, 256:272], g_sm[:, 0:16]), incs=[(GPP, 1)])
            b.emit("vector", lambda e: e.tensor_scalar(
                g_sm[:, 16:32], g_sm[:, 0:16], -1.0, None, op0=ALU.mult),
                incs=[(GPP, 1)])
            ngpp = b.now(GPP)
            for ti in range(NT):
                b.emit("scalar", lambda e, ti=ti: e.activation(
                    g_exp[:, ti * 16:(ti + 1) * 16], p0[:, ti * 16:(ti + 1) * 16],
                    AF.Exp, bias=g_sm[:, 16 + ti:17 + ti], scale=1.0),
                    waits=[(GPP, ngpp)] if ti == 0 else [],
                    incs=[(GPP, 1)] if ti == NT - 1 else [], chain=False)
            b.emit("vector", lambda e: e.reduce_sum(
                g_sm[:, 32:48], g_exp[:].rearrange("p (t e) -> p t e", t=16), axis=AX),
                waits=[(GPP, b.now(GPP))], incs=[(GPP, 1)])
            b.emit("vector", lambda e: e.reciprocal(
                g_sm[:, 48:64], g_sm[:, 32:48]), incs=[(GPP, 1)])
            for ti in range(NT):
                b.emit("vector", lambda e, ti=ti: e.tensor_scalar(
                    g_orig[:, ti * 16:(ti + 1) * 16], g_exp[:, ti * 16:(ti + 1) * 16],
                    g_sm[:, 48 + ti:49 + ti], None, op0=ALU.mult),
                    incs=[(GPP, 1)])

            def v4(a):
                return a.rearrange("p (t g f) -> p t g f", t=16, g=4)

            def v3(a):
                return a.rearrange("p (t g) -> p t g", t=16)

            def e3(a):
                return a.rearrange("p (t e) -> p t e", t=16)

            def bc3(a, w):
                return a.rearrange("p (t o) -> p t o", o=1).broadcast_to([128, 16, w])

            def bc4(a):
                return a.rearrange("p (t g o) -> p t g o", t=16, g=4).broadcast_to(
                    [128, 16, 4, 4])

            b.emit("vector", lambda e: e.tensor_tensor(
                e3(g_sc2[:]), e3(g_orig[:]),
                gbb[:].rearrange("p (o e) -> p o e", o=1).broadcast_to([128, 16, 16]),
                op=ALU.add), incs=[(GPP, 1)])
            b.emit("vector", lambda e: e.reduce_max(
                v3(g_m1g[:]), v4(g_sc2[:]), axis=AX), incs=[(GPP, 1)])
            b.emit("vector", lambda e: e.tensor_tensor(
                v4(g_kn[:]), v4(g_sc2[:]), bc4(g_m1g[:]), op=ALU.is_ge),
                incs=[(GPP, 1)])
            b.emit("vector", lambda e: e.tensor_scalar(
                g_kn[:], g_kn[:], 1e30, None, op0=ALU.mult), incs=[(GPP, 1)])
            b.emit("vector", lambda e: e.tensor_sub(
                g_kn[:], g_sc2[:], g_kn[:]), incs=[(GPP, 1)])
            b.emit("vector", lambda e: e.reduce_max(
                v3(g_gs[:]), v4(g_kn[:]), axis=AX), incs=[(GPP, 1)])
            b.emit("vector", lambda e: e.tensor_add(
                g_gs[:], g_gs[:], g_m1g[:]), incs=[(GPP, 1)])
            b.emit("vector", lambda e: e.reduce_max(
                g_sm[:, 0:16], v3(g_gs[:]), axis=AX), incs=[(GPP, 1)])
            b.emit("vector", lambda e: e.tensor_tensor(
                v3(g_t[:]), v3(g_gs[:]), bc3(g_sm[:, 0:16], 4), op=ALU.is_ge),
                incs=[(GPP, 1)])
            b.emit("vector", lambda e: e.tensor_scalar(
                g_t[:], g_t[:], 1e30, None, op0=ALU.mult), incs=[(GPP, 1)])
            b.emit("vector", lambda e: e.tensor_sub(
                g_t[:], g_gs[:], g_t[:]), incs=[(GPP, 1)])
            b.emit("vector", lambda e: e.reduce_max(
                g_sm[:, 32:48], v3(g_t[:]), axis=AX), incs=[(GPP, 1)])
            b.emit("vector", lambda e: e.tensor_tensor(
                v3(g_t[:]), v3(g_gs[:]), bc3(g_sm[:, 32:48], 4), op=ALU.is_ge),
                incs=[(GPP, 1)])
            b.emit("vector", lambda e: e.tensor_scalar(
                g_t[:], g_t[:], 1e30, None, op0=ALU.mult), incs=[(GPP, 1)])
            b.emit("vector", lambda e: e.tensor_scalar(
                g_t[:], g_t[:], -1e30, None, op0=ALU.add), incs=[(GPP, 1)])
            b.emit("vector", lambda e: e.tensor_tensor(
                v4(g_v2[:]), v4(g_sc2[:]), bc4(g_t[:]), op=ALU.add), incs=[(GPP, 1)])
            b.emit("vector", lambda e: e.reduce_max(
                g_sm[:, 0:16], e3(g_v2[:]), axis=AX), incs=[(GPP, 1)])
            b.emit("vector", lambda e: e.tensor_tensor(
                e3(g_kn[:]), e3(g_v2[:]), bc3(g_sm[:, 0:16], 16), op=ALU.is_ge),
                incs=[(GPP, 1)])
            b.emit("vector", lambda e: e.tensor_scalar(
                g_kn[:], g_kn[:], 1e30, None, op0=ALU.mult), incs=[(GPP, 1)])
            b.emit("vector", lambda e: e.tensor_sub(
                g_kn[:], g_v2[:], g_kn[:]), incs=[(GPP, 1)])
            b.emit("vector", lambda e: e.reduce_max(
                g_sm[:, 32:48], e3(g_kn[:]), axis=AX), incs=[(GPP, 1)])
            b.emit("vector", lambda e: e.tensor_tensor(
                e3(g_kn[:]), e3(g_v2[:]), bc3(g_sm[:, 32:48], 16), op=ALU.is_ge),
                incs=[(GPP, 1)])
            b.emit("vector", lambda e: e.tensor_mul(
                g_Cf[:], g_kn[:], g_orig[:]), incs=[(GPP, 1)])
            ngv = b.now(GPP)
            cbe_hist = {}
            for w in range(4):
                bank = pb[4 + (w % 2)]
                for j in range(4):
                    ti = w * 4 + j
                    ww = []
                    if ti == 0:
                        ww.append((GPP, ngv))
                    if w >= 2 and j == 0:
                        ww.append((CBE, cbe_hist[w - 2]))
                    b.emit("tensor", lambda e, bank=bank, j=j, ti=ti: e.transpose(
                        bank[0:16, j * 128:(j + 1) * 128],
                        g_Cf[:, ti * 16:(ti + 1) * 16], idn[:]),
                        waits=ww, incs=[(CBD, 1)] if j == 3 else [])
                b.emit("vector", lambda e, bank=bank, w=w: e.tensor_scalar(
                    C_km[0:16, w * 512:(w + 1) * 512], bank[0:16, 0:512],
                    0.5, None, op0=ALU.mult),
                    waits=[(CBD, b.now(CBD))], incs=[(CBE, 1)], chain=False)
                cbe_hist[w] = b.now(CBE)

        def ffn_phase(nunits=NU, bias=True):
            gate_phase()
            n_ckm = b.now(CBE)
            yf0 = b.now(YF)
            ad0 = b.now(AD)
            nxat = b.now(XAT)
            for u in range(nunits):
                slot = u % 2
                if u < 2:
                    wts = [(XAT, nxat)] if nxat > 0 else []
                else:
                    wts = [(YF, yf0 + 32 * (u - 1))]
                if u < NE:
                    src1 = we1[u].rearrange("(c p) i -> p c i", p=128)
                    src3 = we3[u].rearrange("(c p) i -> p c i", p=128)
                    src2 = we2[u].rearrange("(c p) d -> p c d", p=128)
                else:
                    hh_ = u - NE
                    src1 = ws1[:, hh_ * 512:(hh_ + 1) * 512].rearrange("(c p) i -> p c i", p=128)
                    src3 = ws3[:, hh_ * 512:(hh_ + 1) * 512].rearrange("(c p) i -> p c i", p=128)
                    src2 = ws2[hh_ * 512:(hh_ + 1) * 512, :].rearrange("(c p) d -> p c d", p=128)
                b.emit("gpsimd", lambda e, s=src1, slot=slot: e.dma_start(W1(slot), s),
                       waits=wts, incs=[(WL[slot], 16)])
                b.emit("gpsimd", lambda e, s=src3, slot=slot: e.dma_start(W3(slot), s),
                       incs=[(WL[slot], 16)])
                b.emit("gpsimd", lambda e, s=src2, slot=slot: e.dma_start(W2(slot), s),
                       incs=[(WL[slot], 16)])
                n_wld = b.now(WL[slot])

                if u < NE:
                    for j in range(4):
                        b.emit("tensor", lambda e, u=u, j=j: e.matmul(
                            pb[j][:, 0:512],
                            onehot[0:16, u * 128:(u + 1) * 128],
                            C_km[0:16, j * 512:(j + 1) * 512],
                            start=True, stop=True),
                            waits=([(CBE, n_ckm), (YF, b.now(YF))] if j == 0 else []),
                            incs=[(CBD, 1)])
                    ncb = b.now(CBD)
                    for j in range(4):
                        b.emit("vector", lambda e, slot=slot, j=j: e.tensor_copy(
                            CBC(slot)[:, j * 512:(j + 1) * 512], pb[j][:, 0:512]),
                            waits=[(CBD, ncb)] if j == 0 else [], incs=[(CBE, 1)],
                            chain=False)
                n_cbe = b.now(CBE)

                for tg in range(NTG):
                    tgc = tg * TG
                    for ic in range(4):
                        h1b = pb[4 + (ic % 2)]
                        h3b = pb[6 + (ic % 2)]
                        icg = (u * NTG + tg) * 4 + ic
                        wh = []
                        if ic == 0 and tg == 0:
                            wh.append((WL[slot], n_wld))
                        if icg >= 2:
                            wh.append((AD, ad0 + 2 * (icg - 1)))
                        for dc in range(NDC):
                            b.emit("tensor", lambda e, h1b=h1b, slot=slot, dc=dc, ic=ic, tgc=tgc: e.matmul(
                                h1b[:, 0:TG],
                                W1(slot)[:, dc, ic * 128:(ic + 1) * 128],
                                xn_fm[:, dc, tgc:tgc + TG],
                                start=(dc == 0), stop=(dc == NDC - 1)),
                                waits=wh if dc == 0 else [],
                                incs=[(HD, 1)] if dc == NDC - 1 else [])
                        for dc in range(NDC):
                            b.emit("tensor", lambda e, h3b=h3b, slot=slot, dc=dc, ic=ic, tgc=tgc: e.matmul(
                                h3b[:, 0:TG],
                                W3(slot)[:, dc, ic * 128:(ic + 1) * 128],
                                xn_fm[:, dc, tgc:tgc + TG],
                                start=(dc == 0), stop=(dc == NDC - 1)),
                                incs=[(HD, 1)] if dc == NDC - 1 else [])
                        nhd = b.now(HD)
                        if u < NE:
                            bia1 = bec1[:, u, ic:ic + 1]
                            bia3 = bec3[:, u, ic:ic + 1]
                            sc3 = 1.0
                        else:
                            jj = (u - NE) * 4 + ic
                            bia1 = bsc1[:, jj:jj + 1]
                            bia3 = bsc3[:, jj:jj + 1]
                            sc3 = 0.5
                        sil = silu_r[:, icg % 2, :]
                        h3s = h3s_r[:, icg % 2, :]
                        b.emit("scalar", lambda e, h1b=h1b, sil=sil, bia1=bia1: e.activation(
                            sil, h1b[:, 0:TG], AF.Silu, bias=bia1, scale=1.0),
                            waits=[(HD, nhd - 1), (HSF, b.now(HSF) - 1)],
                            incs=[(AD, 1)], chain=False)
                        b.emit("scalar", lambda e, h3b=h3b, h3s=h3s, bia3=bia3, sc3=sc3: e.activation(
                            h3s, h3b[:, 0:TG], AF.Identity, bias=bia3, scale=sc3),
                            waits=[(HD, nhd)], incs=[(AD, 1)], chain=False)
                        nad = b.now(AD)
                        if u < NE:
                            b.emit("vector", lambda e, h3s=h3s, slot=slot, tgc=tgc, ic=ic: e.tensor_mul(
                                HS(slot)[:, ic * TG:(ic + 1) * TG], h3s,
                                CBC(slot)[:, tgc:tgc + TG]),
                                waits=[(AD, nad), (CBE, n_cbe)], chain=False)
                            b.emit("vector", lambda e, sil=sil, slot=slot, ic=ic: e.tensor_mul(
                                HS(slot)[:, ic * TG:(ic + 1) * TG],
                                HS(slot)[:, ic * TG:(ic + 1) * TG], sil),
                                incs=[(HSF, 1)])
                        else:
                            b.emit("vector", lambda e, h3s=h3s, sil=sil, slot=slot, ic=ic: e.tensor_mul(
                                HS(slot)[:, ic * TG:(ic + 1) * TG], h3s, sil),
                                waits=[(AD, nad)], incs=[(HSF, 1)], chain=False)
                    nhsf = b.now(HSF)
                    for ic in range(4):
                        for tc in range(2):
                            for dh in range(2):
                                yb = pb[tc * 2 + dh]
                                w = []
                                if tc == 0 and dh == 0:
                                    w = [(HSF, nhsf - 3 + ic)]
                                    if ic == 0:
                                        w.append((YF, b.now(YF)))
                                        if u < NE:
                                            w.append((CBE, n_cbe))
                                b.emit("tensor", lambda e, yb=yb, slot=slot, ic=ic, tc=tc, dh=dh: e.matmul(
                                    yb[:, 0:512],
                                    HS(slot)[:, ic * TG + tc * 128: ic * TG + (tc + 1) * 128],
                                    W2(slot)[:, ic, dh * 512:(dh + 1) * 512],
                                    start=(ic == 0), stop=(ic == 3)),
                                    waits=w,
                                    incs=[(YD, 1)] if ic == 3 else [])
                    nyd = b.now(YD)
                    for tc in range(2):
                        for dh in range(2):
                            yb = pb[tc * 2 + dh]
                            xt = x_sb[:, (tg * 2 + tc) * D + dh * 512:
                                      (tg * 2 + tc) * D + (dh + 1) * 512]
                            b.emit("vector", lambda e, yb=yb, xt=xt: e.tensor_tensor(
                                xt, yb[:, 0:512], xt, op=ALU.add),
                                waits=[(YD, nyd - 3 + tc * 2 + dh)], incs=[(YF, 1)],
                                chain=False)
            if not bias:
                return
            for tg in range(NTG):
                for tc in range(2):
                    for dh in range(2):
                        yb = pb[tc * 2 + dh]
                        t0 = tg * TG + tc * 128
                        w = [(YF, b.now(YF))] if (tc == 0 and dh == 0) else []
                        b.emit("tensor", lambda e, yb=yb, t0=t0, dh=dh: e.matmul(
                            yb[:, 0:512], C_km[0:16, t0:t0 + 128],
                            be2b[0:16, dh * 512:(dh + 1) * 512],
                            start=True, stop=False), waits=w)
                        b.emit("tensor", lambda e, yb=yb, dh=dh: e.matmul(
                            yb[:, 0:512], onesbf[0:1, 0:128],
                            bs2row[0:1, dh * 512:(dh + 1) * 512],
                            start=False, stop=True), incs=[(YD, 1)])
                nyd = b.now(YD)
                for tc in range(2):
                    for dh in range(2):
                        yb = pb[tc * 2 + dh]
                        xt = x_sb[:, (tg * 2 + tc) * D + dh * 512:
                                  (tg * 2 + tc) * D + (dh + 1) * 512]
                        b.emit("vector", lambda e, yb=yb, xt=xt: e.tensor_tensor(
                            xt, yb[:, 0:512], xt, op=ALU.add),
                            waits=[(YD, nyd - 3 + tc * 2 + dh)], incs=[(YF, 1)],
                            chain=False)

        def attn_phase():
            nyf = b.now(YF)
            qkd0 = b.now(QKD)
            b.emit("gpsimd", lambda e: e.dma_start(
                WT(0), wqT.rearrange("(c p) o -> p c o", p=128)),
                waits=[(YF, nyf)], incs=[(WL[0], 16)])
            b.emit("gpsimd", lambda e: e.dma_start(
                WT(1), wkT.rearrange("(c p) o -> p c o", p=128)), incs=[(WL[1], 16)])
            wld_for = {0: (WL[0], b.now(WL[0])), 1: (WL[1], b.now(WL[1]))}
            qkl_hist = {}
            scratch = [qs_d, ks_d]
            for mi in range(3):
                if mi == 2:
                    b.emit("gpsimd", lambda e: e.dma_start(
                        WT(0), wvT.rearrange("(c p) o -> p c o", p=128)),
                        waits=[(QKD, qkd0 + 32)], incs=[(WL[0], 16)])
                    wld_for[2] = (WL[0], b.now(WL[0]))
                for ti in range(NT):
                    bk0 = pb[(ti % 2) * 2]
                    bk1 = pb[(ti % 2) * 2 + 1]
                    for oh in range(2):
                        bk = bk0 if oh == 0 else bk1
                        for dc in range(NDC):
                            w = []
                            if oh == 0 and dc == 0:
                                if ti == 0:
                                    w = [wld_for[mi], (TS[0], b.now(TS[0])),
                                         (TS[1], b.now(TS[1])),
                                         (YF, nyf), (XAT, b.now(XAT))]
                                key2 = ((mi, ti - 2) if ti >= 2 else
                                        ((mi - 1, NT - 2 + ti) if mi >= 1 else None))
                                if key2 is not None and key2 in qkl_hist:
                                    w.append((QKL, qkl_hist[key2]))
                            b.emit("tensor", lambda e, bk=bk, mi=mi, dc=dc, ti=ti, oh=oh: e.matmul(
                                bk[:, 0:512], xn_fm[:, dc, ti * 128:(ti + 1) * 128],
                                WT(mi % 2)[:, dc, oh * 512:(oh + 1) * 512],
                                start=(dc == 0), stop=(dc == NDC - 1)),
                                waits=w,
                                incs=[(QKD, 1)] if dc == NDC - 1 else [])
                    nqkd = b.now(QKD)
                    if mi < 2:
                        gb_ = gqb if mi == 0 else gkb
                        bb_ = bqb if mi == 0 else bkb
                        pv0 = bk0[:, 0:512].rearrange("p (s d) -> p s d", s=8)
                        pv1 = bk1[:, 0:512].rearrange("p (s d) -> p s d", s=8)
                        b.emit("vector", lambda e, pv0=pv0: e.reduce_sum(
                            qs1[:, 0:8], pv0, axis=AX),
                            waits=[(QKD, nqkd)], incs=[(QKL, 1)])
                        b.emit("vector", lambda e, pv1=pv1: e.reduce_sum(
                            qs1[:, 8:16], pv1, axis=AX), incs=[(QKL, 1)])
                        sq0 = scr32[:, 0:512]
                        sq1 = scr32[:, 512:1024]
                        b.emit("scalar", lambda e, bk0=bk0, sq0=sq0: e.activation(
                            sq0, bk0[:, 0:512], AF.Square),
                            waits=[(QKD, nqkd), (QKL, b.now(QKL))], incs=[(QKL, 1)],
                            chain=False)
                        b.emit("scalar", lambda e, bk1=bk1, sq1=sq1: e.activation(
                            sq1, bk1[:, 0:512], AF.Square), incs=[(QKL, 1)],
                            chain=False)
                        nq2 = b.now(QKL)
                        b.emit("vector", lambda e, sq0=sq0: e.reduce_sum(
                            qs2[:, 0:8], sq0.rearrange("p (s d) -> p s d", s=8), axis=AX),
                            waits=[(QKL, nq2)], incs=[(QKL, 1)])
                        b.emit("vector", lambda e, sq1=sq1: e.reduce_sum(
                            qs2[:, 8:16], sq1.rearrange("p (s d) -> p s d", s=8), axis=AX),
                            incs=[(QKL, 1)])
                        b.emit("vector", lambda e: e.tensor_scalar(
                            qm[:, 0:16], qs1[:, 0:16], 1.0 / DH, None, op0=ALU.mult),
                            incs=[(QKL, 1)])
                        b.emit("vector", lambda e: e.tensor_scalar(
                            qr[:, 0:16], qs2[:, 0:16], 1.0 / DH, None, op0=ALU.mult),
                            incs=[(QKL, 1)])
                        b.emit("vector", lambda e: e.tensor_tensor(
                            qmsq[:, 0:16], qm[:, 0:16], qm[:, 0:16], op=ALU.mult),
                            incs=[(QKL, 1)])
                        b.emit("vector", lambda e: e.tensor_sub(
                            qr[:, 0:16], qr[:, 0:16], qmsq[:, 0:16]), incs=[(QKL, 1)])
                        b.emit("vector", lambda e: e.tensor_scalar(
                            qr[:, 0:16], qr[:, 0:16], EPS, None, op0=ALU.add),
                            incs=[(QKL, 1)])
                        b.emit("scalar", lambda e: e.activation(
                            qr[:, 0:16], qr[:, 0:16], AF.Sqrt),
                            waits=[(QKL, b.now(QKL))], incs=[(QKL, 1)], chain=False)
                        b.emit("vector", lambda e: e.reciprocal(
                            qr[:, 0:16], qr[:, 0:16]),
                            waits=[(QKL, b.now(QKL))], incs=[(QKL, 1)])
                        t1 = scr32[:, 0:1024]
                        for s in range(16):
                            bk = bk0 if s < 8 else bk1
                            so = (s % 8) * 64
                            b.emit("vector", lambda e, bk=bk, so=so, s=s, t1=t1: e.tensor_scalar(
                                t1[:, s * 64:(s + 1) * 64], bk[:, so:so + 64],
                                qm[:, s:s + 1], qr[:, s:s + 1],
                                op0=ALU.subtract, op1=ALU.mult),
                                incs=[(QKL, 1)] if s == 15 else [], chain=(s < 2))
                        wtm = [(TS[ti % 2], b.now(TS[ti % 2]))]
                        b.emit("vector", lambda e, t1=t1, gb_=gb_: e.tensor_mul(
                            t1.rearrange("p (s d) -> p s d", s=16),
                            t1.rearrange("p (s d) -> p s d", s=16),
                            gb_[:].rearrange("p (o d) -> p o d", o=1).broadcast_to([128, 16, DH])),
                            incs=[(QKL, 1)])
                        tmt = tmr[:, ti % 2, :]
                        b.emit("vector", lambda e, t1=t1, tmt=tmt, bb_=bb_: e.tensor_add(
                            tmt.rearrange("p (s d) -> p s d", s=16),
                            t1.rearrange("p (s d) -> p s d", s=16),
                            bb_[:].rearrange("p (o d) -> p o d", o=1).broadcast_to([128, 16, DH])),
                            waits=wtm, incs=[(QKL, 1)])
                        nql = b.now(QKL)
                        p = ti % 2
                        fms = fmstg[p]
                        for dc in range(NDC):
                            b.emit("sync", lambda e, tmt=tmt, dc=dc, fms=fms: e.dma_start_transpose(
                                fms[:, dc * 128:(dc + 1) * 128],
                                tmt[:, dc * 128:(dc + 1) * 128]),
                                waits=([(QKL, nql), (SP[p], b.now(SP[p]))]
                                       if dc == 0 else []),
                                incs=[(TS[p], 16)])
                        dst = scratch[mi]
                        b.emit("sync", lambda e, dst=dst, ti=ti, fms=fms: e.dma_start(
                            dst.rearrange("(c o) t -> o c t", o=128)[:, :, ti * 128:(ti + 1) * 128],
                            fms[:].rearrange("p (c t) -> p c t", c=8)),
                            waits=[(TS[p], b.now(TS[p]))], incs=[(SP[p], 16)])
                        qkl_hist[(mi, ti)] = b.now(QKL)
                    else:
                        w = [(QKD, nqkd), (VSP, b.now(VSP))]
                        b.emit("vector", lambda e, bk0=bk0: e.tensor_copy(
                            vstg[:, 0:512], bk0[:, 0:512]),
                            waits=w, incs=[(QKL, 1)])
                        b.emit("vector", lambda e, bk1=bk1: e.tensor_copy(
                            vstg[:, 512:1024], bk1[:, 0:512]),
                            incs=[(QKL, 1)])
                        b.emit("sync", lambda e, ti=ti: e.dma_start(
                            vs_d[ti * 128:(ti + 1) * 128, :], vstg[:, 0:1024]),
                            waits=[(QKL, b.now(QKL))], incs=[(VSP, 16)])
                        qkl_hist[(2, ti)] = b.now(QKL)
            b.emit("gpsimd", lambda e: e.dma_start(
                WT(1), woT.rearrange("(c p) o -> p c o", p=128)),
                waits=[(QKD, qkd0 + 96)], incs=[(WL[1], 16)])
            wld_wo = (WL[1], b.now(WL[1]))

            n_spa = b.now(SP[0]); n_spb = b.now(SP[1]); n_vsp = b.now(VSP)

            avd_hist = {}
            hpl_hist = {}
            one_hist = {}
            exd_bank = {}
            pfr_slot = {}
            rcd_o = 0

            def prefetch_pair(hp):
                slot = hp % 2
                w = [(SP[0], n_spa), (SP[1], n_spb), (VSP, n_vsp)]
                if hp >= 2:
                    w.append((AVD, avd_hist[hp - 2]))
                b.emit("sync", lambda e, hp=hp, slot=slot: e.dma_start(
                    QPair(slot), qs_d[hp * 128:(hp + 1) * 128, :]),
                    waits=w, incs=[(HP[slot], 16)])
                b.emit("sync", lambda e, hp=hp, slot=slot: e.dma_start(
                    KPair(slot), ks_d[hp * 128:(hp + 1) * 128, :]),
                    incs=[(HP[slot], 16)])
                b.emit("sync", lambda e, hp=hp, slot=slot: e.dma_start(
                    VPair(slot)[:, :, 0:64],
                    vs_d.rearrange("(c p) d -> p c d", p=128)[:, :, hp * 128: hp * 128 + 64]),
                    incs=[(HP[slot], 16)])
                b.emit("sync", lambda e, hp=hp, slot=slot: e.dma_start(
                    VPair(slot)[:, :, 65:129],
                    vs_d.rearrange("(c p) d -> p c d", p=128)[:, :, hp * 128 + 64: hp * 128 + 128]),
                    incs=[(HP[slot], 16)])
                hpl_hist[hp] = (HP[slot], b.now(HP[slot]))
                b.emit("vector", lambda e, slot=slot: e.memset(
                    VPair(slot)[:, :, 64:65], 1.0),
                    waits=[hpl_hist[hp]], incs=[(SMK, 1)])
                b.emit("vector", lambda e, slot=slot: e.memset(
                    VPair(slot)[:, :, 129:130], 1.0), incs=[(SMK, 1)])
                one_hist[hp] = b.now(SMK)

            prefetch_pair(0)
            for hp in range(NH // 2):
                slot = hp % 2
                if hp + 1 < NH // 2:
                    prefetch_pair(hp + 1)
                for hh in range(2):
                    hof = hh * 64
                    for q in range(NT):
                        W_ = (q + 1) * 128
                        nchunks = (W_ + 511) // 512
                        for j in range(nchunks):
                            wdt = min(512, W_ - j * 512)
                            wv = []
                            if j == 0:
                                wv = [(SMK, b.now(SMK))]
                                if q == 0:
                                    wv.append(hpl_hist[hp])
                                    wv.append((SMK, one_hist[hp]))
                            b.emit("tensor", lambda e, j=j, q=q, wdt=wdt, slot=slot, hof=hof: e.matmul(
                                pb[j][:, 0:wdt],
                                QPair(slot)[hof:hof + 64, q * 128:(q + 1) * 128],
                                KPair(slot)[hof:hof + 64, j * 512:j * 512 + wdt],
                                start=True, stop=True),
                                waits=wv,
                                incs=[(STM, 1)] if j == nchunks - 1 else [])
                        nstm = b.now(STM)
                        jd = q // 4
                        cof = (q % 4) * 128
                        b.emit("vector", lambda e, jd=jd, cof=cof: e.tensor_tensor(
                            pb[jd][:, cof:cof + 128], pb[jd][:, cof:cof + 128],
                            mtm[:], op=ALU.add),
                            waits=[(STM, nstm)], incs=[(SMK, 1)], chain=False)
                        for j in range(nchunks):
                            wdt = min(512, W_ - j * 512)
                            b.emit("vector", lambda e, j=j, wdt=wdt: e.reduce_max(
                                tmp4[:, j:j + 1], pb[j][:, 0:wdt], axis=AX),
                                incs=[(SMK, 1)])
                        b.emit("vector", lambda e, q=q, nchunks=nchunks: e.tensor_reduce(
                            nmc[:, q:q + 1], tmp4[:, 0:nchunks], axis=AX,
                            op=ALU.max, negate=True),
                            incs=[(SMK, 1)])
                    nsm = b.now(SMK)
                    b.emit("tensor", lambda e: e.transpose(
                        pb[7][0:16, 0:128], nmc[:, 0:16], idn[:]),
                        waits=[(SMK, nsm), (RCD, rcd_o)], incs=[(NMT, 1)])
                    b.emit("vector", lambda e: e.tensor_copy(
                        nmT_sb[0:16, 0:128], pb[7][0:16, 0:128]),
                        waits=[(NMT, b.now(NMT))], incs=[(NMT, 1)])
                    b.emit("sync", lambda e: e.dma_start(
                        nmrow[0:1, :].rearrange("o (c t) -> o c t", c=16),
                        nmT_sb[0:16, 0:128]),
                        waits=[(NMT, b.now(NMT))], incs=[(NMT, 16)])
                    n_nmrow = b.now(NMT)

                    tiles = [(g, c) for g in range(4) for c in range(4 * (g + 1))]

                    def emit_preload_sp(g, c):
                        sb_ = pb[4 + (c % 2)]
                        w = [(NMT, n_nmrow)] if (g == 0 and c == 0) else []
                        if (c % 2) in exd_bank:
                            w.append((EXD, exd_bank[(c % 2)]))
                        b.emit("tensor", lambda e, sb_=sb_, g=g: e.matmul(
                            sb_[:, 0:512], onesbf[0:1, 0:128],
                            nmrow[0:1, g * 512:(g + 1) * 512],
                            start=True, stop=False), waits=w)
                        b.emit("tensor", lambda e, sb_=sb_, c=c, g=g, slot=slot, hof=hof: e.matmul(
                            sb_[:, 0:512],
                            KPair(slot)[hof:hof + 64, c * 128:(c + 1) * 128],
                            QPair(slot)[hof:hof + 64, g * 512:(g + 1) * 512],
                            start=False, stop=True), incs=[(SKM, 1)])

                    def emit_expav(g, c, last_in_group):
                        sb_ = pb[4 + (c % 2)]
                        pslot = c % 4
                        nskm = skm_of[(g, c)]
                        r = c - 4 * g
                        wexp = [(SKM, nskm)]
                        if r >= 0:
                            b.emit("vector", lambda e, sb_=sb_, r=r: e.tensor_tensor(
                                sb_[:, r * 128:(r + 1) * 128],
                                sb_[:, r * 128:(r + 1) * 128], mut[:], op=ALU.add),
                                waits=[(SKM, nskm)], incs=[(KMM, 1)], chain=False)
                            wexp.append((KMM, b.now(KMM)))
                        if pslot in pfr_slot:
                            wexp.append((AVD, pfr_slot[pslot]))
                        b.emit("scalar", lambda e, sb_=sb_, pslot=pslot: e.activation(
                            PP(pslot), sb_[:, 0:512], AF.Exp),
                            waits=wexp, incs=[(EXD, 1)], chain=False)
                        if r > 0:
                            b.emit("scalar", lambda e, pslot=pslot, r=r: e.memzero(
                                PP(pslot)[:, 0:r * 128]), incs=[(EXD, 1)])
                        exd_bank[(c % 2)] = b.now(EXD)
                        exd_of[(g, c)] = b.now(EXD)

                    def emit_av(g, c, nch):
                        pslot = c % 4
                        w = [(EXD, exd_of[(g, c)])]
                        if c == 0:
                            w.append((RCD, rcd_o))
                            w.append((SMK, one_hist[hp]))
                        b.emit("tensor", lambda e, slot=slot, hh=hh, c=c, pslot=pslot, nch=nch: e.matmul(
                            pb[6][0:65, 0:512],
                            VPair(slot)[:, c, hh * 65:(hh + 1) * 65],
                            PP(pslot),
                            start=(c == 0), stop=(c == nch - 1)),
                            waits=w,
                            incs=[(AVD, 1)])
                        pfr_slot[pslot] = b.now(AVD)

                    def emit_group_end(g):
                        nonlocal rcd_o
                        navd = b.now(AVD)
                        b.emit("vector", lambda e: e.reciprocal(
                            rrow32[0:1, 0:512], pb[6][64:65, 0:512]),
                            waits=[(AVD, navd)], incs=[(RCD, 1)])
                        b.emit("vector", lambda e: e.tensor_copy(
                            rrow16[0:1, 0:512], rrow32[0:1, 0:512]),
                            incs=[(RCD, 1)])
                        b.emit("tensor", lambda e: e.matmul(
                            pb[7][0:64, 0:512], onesbf[0:1, 0:64], rrow16[0:1, 0:512],
                            start=True, stop=True),
                            waits=[(RCD, b.now(RCD))], incs=[(RCD, 1)])
                        b.emit("vector", lambda e: e.tensor_copy(
                            rb32[0:64, 0:512], pb[7][0:64, 0:512]),
                            waits=[(RCD, b.now(RCD))], incs=[(RCD, 1)])
                        b.emit("vector", lambda e, hp=hp, hof=hof, g=g: e.tensor_mul(
                            xn_fm[hof:hof + 64, hp, g * 512:(g + 1) * 512],
                            pb[6][0:64, 0:512], rb32[0:64, 0:512]),
                            waits=[(RCD, b.now(RCD))], incs=[(RCD, 1)])
                        rcd_o = b.now(RCD)

                    skm_of = {}
                    exd_of = {}
                    prev = None
                    for idx, (g, c) in enumerate(tiles):
                        emit_preload_sp(g, c)
                        skm_of[(g, c)] = b.now(SKM)
                        if prev is not None:
                            pg, pc = prev
                            nch_p = 4 * (pg + 1)
                            emit_expav(pg, pc, pc == nch_p - 1)
                            emit_av(pg, pc, nch_p)
                            if pc == nch_p - 1:
                                emit_group_end(pg)
                        prev = (g, c)

                    pg, pc = prev
                    nch_p = 4 * (pg + 1)
                    emit_expav(pg, pc, True)
                    emit_av(pg, pc, nch_p)
                    emit_group_end(pg)
                avd_hist[hp] = b.now(AVD)

            n_rcd = b.now(RCD)
            xat_hist = {}
            for ti in range(NT):
                bk0 = pb[(ti % 2) * 2]
                bk1 = pb[(ti % 2) * 2 + 1]
                for oh in range(2):
                    bk = bk0 if oh == 0 else bk1
                    for dc in range(NDC):
                        w = []
                        if oh == 0 and dc == 0:
                            if ti == 0:
                                w = [wld_wo, (RCD, n_rcd)]
                            if ti >= 2:
                                w.append((XAT, xat_hist[ti - 2]))
                        b.emit("tensor", lambda e, bk=bk, dc=dc, ti=ti, oh=oh: e.matmul(
                            bk[:, 0:512], xn_fm[:, dc, ti * 128:(ti + 1) * 128],
                            WT(1)[:, dc, oh * 512:(oh + 1) * 512],
                            start=(dc == 0), stop=False),
                            waits=w)
                    b.emit("tensor", lambda e, bk=bk, oh=oh: e.matmul(
                        bk[:, 0:512], onesbf[0:1, 0:128],
                        borow[0:1, oh * 512:(oh + 1) * 512],
                        start=False, stop=True), incs=[(QKD, 1)])
                nq = b.now(QKD)
                for oh in range(2):
                    bk = bk0 if oh == 0 else bk1
                    xt = x_sb[:, ti * D + oh * 512: ti * D + (oh + 1) * 512]
                    b.emit("vector", lambda e, bk=bk, xt=xt: e.tensor_tensor(
                        xt, bk[:, 0:512], xt, op=ALU.add),
                        waits=[(QKD, nq - 1 + oh)], incs=[(XAT, 1)])
                xat_hist[ti] = b.now(XAT)

        ln_phase()
        if STOP_AFTER == "ln":
            pass
        elif isinstance(STOP_AFTER, str) and STOP_AFTER.startswith("u"):
            ffn_phase(nunits=int(STOP_AFTER[1:]), bias=False)
        else:
            ffn_phase()
            if STOP_AFTER != "ffn1":
                ln_phase()
                attn_phase()
                if STOP_AFTER != "attn":
                    ln_phase()
                    ffn_phase()
        b.emit("sync", lambda e: e.dma_start(
            out.rearrange("(t p) d -> p t d", p=128),
            x_sb[:].rearrange("p (t d) -> p t d", t=NT)),
            waits=[(YF, b.now(YF)), (XAT, b.now(XAT))], incs=[(ST, 16)])
        if DEBUG_DUMP:
            dxn = nc.dram_tensor("dbg_xn", [128, NDC * L], BF16,
                                 kind="ExternalOutput").ap()
            du = nc.dram_tensor("dbg_u16", [128, 37120], BF16,
                                kind="ExternalOutput").ap()
            dg = nc.dram_tensor("dbg_gsc", [128, 1792], F32,
                                kind="ExternalOutput").ap()
            dt_ = nc.dram_tensor("dbg_tmr", [128, 2 * D], BF16,
                                 kind="ExternalOutput").ap()
            ds = nc.dram_tensor("dbg_stat", [128, 192], F32,
                                kind="ExternalOutput").ap()
            wdump = [(TS[0], b.now(TS[0])), (TS[1], b.now(TS[1])),
                     (LNP, b.now(LNP)), (CBE, b.now(CBE)), (GPP, b.now(GPP)),
                     (HSF, b.now(HSF))]
            b.emit("sync", lambda e: e.dma_start(
                dxn.rearrange("p (c t) -> p c t", c=NDC), xn_fm[:]),
                waits=wdump, incs=[(ST, 16)])
            b.emit("sync", lambda e: e.dma_start(du, u16[:]), incs=[(ST, 16)])
            b.emit("sync", lambda e: e.dma_start(dg, gsc[:]), incs=[(ST, 16)])
            b.emit("sync", lambda e: e.dma_start(
                dt_.rearrange("p (s d) -> p s d", s=2), tmr[:]), incs=[(ST, 16)])
            b.emit("sync", lambda e: e.dma_start(ds, stat32[:]), incs=[(ST, 16)])
            dsc = nc.dram_tensor("dbg_scr", [128, 1024], F32,
                                 kind="ExternalOutput").ap()
            b.emit("sync", lambda e: e.dma_start(dsc, scr32[:]), incs=[(ST, 16)])
        b.emit("sync", lambda e: None, waits=[(ST, b.now(ST))])

        import contextlib
        with contextlib.ExitStack() as stack:
            chained = ("vector", "scalar") if not DEBUG_SERIAL else (
                "vector", "scalar", "tensor")
            for nm in chained:
                b.sem("priv_" + nm)
            for nm in b.semnames:
                b.sems[nm] = stack.enter_context(nc.semaphore(nm))

            streams = {"sync": [], "tensor": [], "vector": [], "scalar": [],
                       "gpsimd": []}
            for op in b.ops:
                streams[op.eng].append(op)

            def runner(ops, eng_name):
                priv = (b.sems.get("priv_" + eng_name)
                        if eng_name in chained else None)

                def run(eng):
                    cnt = 0
                    pend = []
                    for op in ops:
                        for s, v in op.waits:
                            eng.wait_ge(b.sems[s], v)
                        if priv is not None and op.chain:
                            for s, v in pend:
                                eng.wait_ge(b.sems[s] if s != "__priv" else priv, v)
                            pend = []
                        inst = op.fn(eng)
                        if inst is None:
                            continue
                        if priv is not None:
                            if op.incs:
                                pend = list(op.incs_after)
                            else:
                                cnt += 1
                                inst.then_inc(priv, 1)
                                pend = [("__priv", cnt)]
                        for s, n in op.incs:
                            inst.then_inc(b.sems[s], n)
                return run

            blk.sync(runner(streams["sync"], "sync"))
            blk.tensor(runner(streams["tensor"], "tensor"))
            blk.vector(runner(streams["vector"], "vector"))
            blk.scalar(runner(streams["scalar"], "scalar"))
            blk.gpsimd(runner(streams["gpsimd"], "gpsimd"))

    return nc


_prog = None


def kernel(**inputs):
    global _prog
    f32 = np.float32

    def cc(a):
        return np.ascontiguousarray(np.asarray(a, dtype=f32))

    x = cc(inputs["L0_attn_x"])
    base = {
        "wqT": cc(np.asarray(inputs["L0_attn_Wq"], dtype=f32).T),
        "wkT": cc(np.asarray(inputs["L0_attn_Wk"], dtype=f32).T),
        "wvT": cc(np.asarray(inputs["L0_attn_Wv"], dtype=f32).T),
        "woT": cc(np.asarray(inputs["L0_attn_Wo"], dtype=f32).T),
        "bo": cc(inputs["L0_attn_bo"]).reshape(1, D),
        "gq_b": cc(np.tile(np.asarray(inputs["L0_attn_gq"], dtype=f32).reshape(1, DH), (128, 1))),
        "bq_b": cc(np.tile(np.asarray(inputs["L0_attn_bq"], dtype=f32).reshape(1, DH), (128, 1))),
        "gk_b": cc(np.tile(np.asarray(inputs["L0_attn_gk"], dtype=f32).reshape(1, DH), (128, 1))),
        "bk_b": cc(np.tile(np.asarray(inputs["L0_attn_bk"], dtype=f32).reshape(1, DH), (128, 1))),
        "gwT": cc(np.asarray(inputs["L0_ffn_gate_w"], dtype=f32).T),
        "gb_b": cc(np.tile(np.asarray(inputs["L0_ffn_gate_b"], dtype=f32).reshape(1, NE), (128, 1))),
        "we1": cc(inputs["L0_ffn_We1"]),
        "we3": cc(inputs["L0_ffn_We3"]),
        "we2": cc(inputs["L0_ffn_We2"]),
        "ws1": cc(inputs["L0_ffn_ws1"]),
        "ws3": cc(inputs["L0_ffn_ws3"]),
        "ws2": cc(inputs["L0_ffn_ws2"]),
        "becol1": cc(np.transpose(
            np.asarray(inputs["L0_ffn_be1"], dtype=f32).T.reshape(4, 128, NE),
            (1, 2, 0))),
        "becol3": cc(np.transpose(
            np.asarray(inputs["L0_ffn_be3"], dtype=f32).T.reshape(4, 128, NE),
            (1, 2, 0))),
        "bscol1": cc(np.asarray(inputs["L0_ffn_bs1"], dtype=f32).reshape(8, 128).T),
        "bscol3": cc(np.asarray(inputs["L0_ffn_bs3"], dtype=f32).reshape(8, 128).T),
        "be2": cc(inputs["L0_ffn_be2"]),
        "bs2": cc(inputs["L0_ffn_bs2"]).reshape(1, D),
        "oneh": cc(np.kron(np.eye(NE), np.ones((1, 128)))),
        "mask_tm": cc(np.where(
            np.arange(128)[None, :] <= np.arange(128)[:, None], 0.0, NEG)),
        "mask_ut": cc(np.where(
            np.arange(128)[None, :] >= np.arange(128)[:, None], 0.0, NEG)),
        "ident": cc(np.eye(128)),
    }
    if _prog is None:
        _prog = build_program()
    in_maps = [dict(base, x=np.ascontiguousarray(x[c])) for c in range(NCORE)]
    res = run_bass_kernel_spmd(_prog, in_maps, core_ids=list(range(NCORE)))
    global _last_results
    _last_results = res.results
    return np.stack([r["out"] for r in res.results], axis=0)
